# version 07 (final)
# baseline (speedup 1.0000x reference)
from concourse.bass_utils import run_bass_kernel_spmd
"""Mini-framework for raw-Bass straight-line programs with explicit sync.

One semaphore per compute engine (inc 1 per instruction) + named DMA
semaphores (inc 16 per dma_start). Tokens = (sem_handle, value) identify
completion points. Each emit auto-incs its engine's sem and waits on the
given dependency tokens (deduped per consumer engine, monotonic).

DMA rule (sim-enforced): completions on one DMA sem are unordered, so a
wait on an intermediate value is only valid if flow control guarantees no
later DMA on that sem was in flight. Use one sem per stream+parity and
standard double-buffer flow control, or wait on the cumulative total.
"""
import numpy as np
import concourse.bass as bass
import concourse.mybir as mybir

F32 = mybir.dt.float32
BF16 = mybir.dt.bfloat16
AF = mybir.ActivationFunctionType
ALU = mybir.AluOpType
AX = mybir.AxisListType

ENGINES = ("sync", "scalar", "tensor", "vector", "gpsimd")


class Sched:
    def __init__(self, nc):
        self.nc = nc
        self.sem = {}
        self.cnt = {}
        for e in ENGINES:
            self.sem[e] = nc.alloc_semaphore(f"se_{e}")
            self.cnt[e] = 0
        self.dsem = {}
        self.dcnt = {}
        self.waited = {e: {} for e in ENGINES}
        self.eng = {e: getattr(nc, e) for e in ENGINES}

    def _wait(self, e, tok):
        if tok is None:
            return
        sem, val = tok
        if val <= 0:
            return
        w = self.waited[e]
        key = id(sem)
        if w.get(key, -1) >= val:
            return
        self.eng[e].wait_ge(sem, val)
        w[key] = val

    def waitall(self, e, toks):
        best = {}
        for t in toks:
            if t is None:
                continue
            sem, val = t
            k = id(sem)
            if k not in best or best[k][1] < val:
                best[k] = (sem, val)
        for t in best.values():
            self._wait(e, t)

    def op(self, e, fn, deps=()):
        self.waitall(e, deps)
        ins = fn(self.eng[e])
        self.cnt[e] += 1
        ins.then_inc(self.sem[e], 1)
        return (self.sem[e], self.cnt[e])

    def dq(self, name):
        if name not in self.dsem:
            self.dsem[name] = self.nc.alloc_semaphore(f"sd_{name}")
            self.dcnt[name] = 0
        return self.dsem[name]

    def dma(self, e, q, out, in_, deps=(), **kw):
        sem = self.dq(q)
        self.waitall(e, deps)
        ins = self.eng[e].dma_start(out=out, in_=in_, **kw)
        self.dcnt[q] += 16
        ins.then_inc(sem, 16)
        return (sem, self.dcnt[q])

    def dtok(self, q):
        sem = self.dq(q)
        return (sem, self.dcnt[q])

    def tok(self, e):
        return (self.sem[e], self.cnt[e])

    def barrier_tokens(self):
        return [self.tok(e) for e in ENGINES] + [self.dtok(q) for q in self.dsem]


"""Emitter library for the mk_hybrid_d2 megakernel.

Conventions:
- Residual stream x: feature-major fp32 SBUF [128, 8*T] (chunk c of d_model at
  cols [c*T,(c+1)*T]); T = tokens per core.
- All matmuls: lhsT = weight [k<=128 partitions, m<=128], rhs = activation
  chunk [k, n<=512]; PSUM accumulation over k-chunks; bias via K=1 matmul
  with ones row.
- bf16 operands for big matmuls (cast at DMA or at evac), fp32 PSUM.
"""
import numpy as np
import concourse.bass as bass
import concourse.mybir as mybir

I32 = mybir.dt.int32


class PsumPool:
    def __init__(self, nc, n, width=512, name="pp"):
        self.banks = [nc.alloc_psum_tensor(f"{name}{i}", [128, width], F32)
                      for i in range(n)]
        self.war = [None] * n
        self.i = 0

    def get(self):
        i = self.i
        self.i = (self.i + 1) % len(self.banks)
        return self.banks[i], self.war[i], i

    def done(self, slot, tok):
        self.war[slot] = tok


class Ctx:
    def __init__(self, nc, T):
        self.nc = nc
        self.s = Sched(nc)
        self.T = T
        self.NT = T // 512
        self.ones_row = None
        self.ones_col = None
        self.ident = None
        self.const_toks = []

    def load_consts(self, ident_dram=None):
        nc, s = self.nc, self.s
        self.ones_row = nc.alloc_sbuf_tensor("c_ones_row", [1, 512], F32)
        self.ones_col = nc.alloc_sbuf_tensor("c_ones_col", [128, 1], F32)
        t1 = s.op("vector", lambda v: v.memset(self.ones_row[:], 1.0))
        t2 = s.op("vector", lambda v: v.memset(self.ones_col[:], 1.0))
        self.const_toks = [t1, t2]
        self._constf = {}
        if ident_dram is not None:
            self.ident = nc.alloc_sbuf_tensor("c_ident", [128, 128], F32)
            self.const_toks.append(s.dma("sync", "consts", self.ident[:], ident_dram))


class LnScratch:
    def __init__(self, nc, name="ln"):
        self.sq = nc.alloc_sbuf_tensor(f"{name}_sq", [128, 1024], F32)
        self.rows = nc.alloc_sbuf_tensor(f"{name}_rows", [1, 4 * 512], F32)
        self.war = []


def constf(ctx, val, parts=1):
    key = (val, parts)
    if key not in ctx._constf:
        t = ctx.nc.alloc_sbuf_tensor(f"cf_{len(ctx._constf)}", [parts, 1], F32)
        tok = ctx.s.op("vector", lambda v: v.memset(t[:], float(val)))
        ctx.const_toks.append(tok)
        ctx._constf[key] = t
    return ctx._constf[key][:, 0:1]


def emit_ln_tile(ctx, t0, n, x_ap_fn, out_fn, pool, scr, *,
                 gamma_row=None, beta_row=None, deps=(), n_chunks=8,
                 dsize=1024.0, eps=1e-5):
    s, nc = ctx.s, ctx.nc
    deps = list(deps) + list(scr.war)
    bank_s, war_s, sl_s = pool.get()
    bank_q, war_q, sl_q = pool.get()
    sq_tok = [None, None]
    last_q_mm = None
    for c in range(n_chunks):
        xa = x_ap_fn(c)
        h = c % 2
        sq = scr.sq[:, h * 512:h * 512 + n]
        sq_tok[h] = s.op("scalar", lambda e, xa=xa, sq=sq: e.activation(
            sq, xa, AF.Square), deps=deps + [sq_tok[h], last_q_mm])
        s.op("tensor", lambda t, xa=xa, c=c: t.matmul(
            bank_s[0:1, 0:n], ctx.ones_col[:], xa,
            start=(c == 0), stop=(c == n_chunks - 1)),
            deps=deps + ctx.const_toks + ([war_s] if c == 0 else []))
        last_q_mm = s.op("tensor", lambda t, c=c, sq=sq: t.matmul(
            bank_q[0:1, 0:n], ctx.ones_col[:], sq,
            start=(c == 0), stop=(c == n_chunks - 1)),
            deps=[sq_tok[h]] + ([war_q] if c == 0 else []))
    mm_tok = s.tok("tensor")
    def rs_(r):
        return scr.rows[0:1, r * 512:r * 512 + n]
    tm = s.op("vector", lambda v: v.tensor_scalar(
        rs_(0), bank_s[0:1, 0:n], 1.0 / dsize, None, ALU.mult), deps=[mm_tok])
    tv = s.op("vector", lambda v: v.tensor_scalar(
        rs_(1), bank_q[0:1, 0:n], 1.0 / dsize, None, ALU.mult), deps=[mm_tok])
    pool.done(sl_s, tm)
    pool.done(sl_q, tv)
    t3 = s.op("vector", lambda v: v.tensor_tensor(
        rs_(2), rs_(0), rs_(0), ALU.mult), deps=[tm])
    t4 = s.op("vector", lambda v: v.tensor_tensor(
        rs_(1), rs_(1), rs_(2), ALU.subtract), deps=[tv, t3])
    eps_ap = constf(ctx, eps)
    t5 = s.op("scalar", lambda e: e.activation(
        rs_(1), rs_(1), AF.Sqrt, bias=eps_ap), deps=[t4] + ctx.const_toks)
    t6 = s.op("vector", lambda v: v.reciprocal(rs_(1), rs_(1)), deps=[t5])
    t7 = s.op("vector", lambda v: v.tensor_tensor(
        rs_(2), rs_(0), rs_(1), ALU.mult), deps=[t6])
    t8 = s.op("vector", lambda v: v.tensor_scalar(
        rs_(2), rs_(2), -1.0, None, ALU.mult), deps=[t7])
    toks = []
    for c in range(n_chunks):
        bank_a, war_a, sl_a = pool.get()
        bank_b, war_b, sl_b = pool.get()
        if gamma_row is None:
            ta = s.op("tensor", lambda t: t.matmul(
                bank_a[:, 0:n], ctx.ones_row[0:1, 0:128], rs_(1),
                start=True, stop=True), deps=[t6, war_a] + ctx.const_toks)
            tb = s.op("tensor", lambda t: t.matmul(
                bank_b[:, 0:n], ctx.ones_row[0:1, 0:128], rs_(2),
                start=True, stop=True), deps=[t8, war_b] + ctx.const_toks)
        else:
            g = gamma_row[0:1, c * 128:(c + 1) * 128]
            b_ = beta_row[0:1, c * 128:(c + 1) * 128]
            ta = s.op("tensor", lambda t, g=g: t.matmul(
                bank_a[:, 0:n], g, rs_(1), start=True, stop=True),
                deps=[t6, war_a])
            s.op("tensor", lambda t, g=g: t.matmul(
                bank_b[:, 0:n], g, rs_(2), start=True, stop=False),
                deps=[t8, war_b])
            tb = s.op("tensor", lambda t, b_=b_: t.matmul(
                bank_b[:, 0:n], b_, ctx.ones_row[0:1, 0:n], start=False,
                stop=True), deps=ctx.const_toks)
        tok = out_fn(c, bank_a[:, 0:n], bank_b[:, 0:n], [ta, tb])
        pool.done(sl_a, tok)
        pool.done(sl_b, tok)
        toks.append(tok)
    scr.war = [s.tok("tensor"), s.tok("scalar"), toks[-1]]
    return toks, mm_tok


def emit_ln_std(ctx, t0, n, x_ap_fn, out_chunk_ap_fn, pool, scr, tmp_chunk,
                **kw):
    s = ctx.s
    war = {"t": None}
    prev = list(scr.war)

    def out_fn(c, A, B, dts):
        xa = x_ap_fn(c)
        t1 = s.op("vector", lambda v: v.tensor_tensor(
            tmp_chunk[:, 0:n], xa, A, ALU.mult), deps=dts + [war["t"]] + prev)
        t2 = s.op("vector", lambda v: v.tensor_tensor(
            out_chunk_ap_fn(c), tmp_chunk[:, 0:n], B, ALU.add), deps=[t1])
        war["t"] = t2
        return t2
    return emit_ln_tile(ctx, t0, n, x_ap_fn, out_fn, pool, scr, **kw)


def wslab_3d(w_dram, k0, nk, n0, nn, p=128):
    return w_dram[k0 * p:(k0 + nk) * p, n0:n0 + nn].rearrange(
        "(k p) n -> p k n", p=p)


class WBuf:
    def __init__(self, nc, name, kchunks, ncols, dtype=BF16, parts=128):
        self.b = [nc.alloc_sbuf_tensor(f"{name}{i}", [parts, kchunks * ncols], dtype)
                  for i in range(2)]
        self.kchunks = kchunks
        self.parts = parts
        self.ncols = ncols
        self.war = [[], []]
        self.ld = [None, None]
        self.i = 0

    def load(self, ctx, q, dram_slab_ap, sub=None):
        i = self.i
        self.i = (self.i + 1) % 2
        dst = self.b[i][:].rearrange("p (k n) -> p k n", k=self.kchunks)
        if sub is not None:
            dst = dst[:, :, 0:sub]
        self.ld[i] = ctx.s.dma("gpsimd", f"{q}{i}", dst,
                               dram_slab_ap, deps=self.war[i])
        self.war[i] = []
        return i

    def kslice(self, i, k):
        return self.b[i][:, k * self.ncols:(k + 1) * self.ncols]

    def used(self, i, tok):
        self.war[i].append(tok)


def emit_l0ffn(ctx, x_ap_fn, w, pool, scr, bufs, deps=()):
    s = ctx.s
    T = ctx.T
    xb, h0, g1 = bufs["xb"], bufs["h0"], bufs["g1"]
    asb, tmp = bufs["asb"], bufs["tmp"]
    wb1, wb2, wbig = bufs["wb1"], bufs["wb2"], bufs["wbig"]
    xb_war = [bufs.get("xb_war")]
    h0_war = [None]
    g1_war = [None]
    res_toks = []
    for it in range(ctx.NT):
        t0 = it * 512
        n = 512
        def xap(c):
            return x_ap_fn(c, t0, n)
        ln_toks, _ = emit_ln_std(
            ctx, t0, n, xap, lambda c: xb[:, c * 512:c * 512 + n],
            pool, scr, tmp, deps=list(deps) + [xb_war[0]])
        xb_tok = ln_toks[-1]
        st1 = []
        for nn in range(32):
            i1 = wb1.load(ctx, "w1", wslab_3d(w["w1T"], 0, 8, nn * 128, 128))
            i2 = wb2.load(ctx, "w2", wslab_3d(w["w2T"], 0, 8, nn * 128, 128))
            ba, wa, sa = pool.get()
            bb, wb_, sb = pool.get()
            for k in range(8):
                s.op("tensor", lambda t, k=k: t.matmul(
                    ba[:, 0:n], wb1.kslice(i1, k), xb[:, k * 512:k * 512 + n],
                    start=(k == 0), stop=(k == 7)),
                    deps=[wb1.ld[i1], ln_toks[k], wa] if k == 0 else [wb1.ld[i1], ln_toks[k]])
            ta = s.tok("tensor")
            for k in range(8):
                s.op("tensor", lambda t, k=k: t.matmul(
                    bb[:, 0:n], wb2.kslice(i2, k), xb[:, k * 512:k * 512 + n],
                    start=(k == 0), stop=(k == 7)),
                    deps=[wb2.ld[i2], wb_] if k == 0 else [wb2.ld[i2]])
            tb = s.tok("tensor")
            wb1.used(i1, ta)
            wb2.used(i2, tb)
            t_silu = s.op("scalar", lambda e, nn=nn: e.activation(
                asb[:, 0:n], ba[:, 0:n], AF.Silu, scale=w["s1"][:, nn:nn + 1]),
                deps=[ta, st1[-1] if st1 else None])
            t_h0 = s.op("vector", lambda v, nn=nn: v.scalar_tensor_tensor(
                h0[:, nn * 512:nn * 512 + n], bb[:, 0:n], w["s2"][:, nn:nn + 1],
                asb[:, 0:n], ALU.mult, ALU.mult), deps=[tb, t_silu, h0_war[0]])
            pool.done(sa, t_silu)
            pool.done(sb, t_h0)
            st1.append(t_h0)
        st2 = []
        for nn in range(32):
            ib = wbig.load(ctx, "h1", wslab_3d(w["h1T"], 0, 32, nn * 128, 128))
            ba, wa, sa = pool.get()
            for k in range(32):
                s.op("tensor", lambda t, k=k: t.matmul(
                    ba[:, 0:n], wbig.kslice(ib, k), h0[:, k * 512:k * 512 + n],
                    start=(k == 0), stop=(k == 31)),
                    deps=[wbig.ld[ib], st1[k], wa] if k == 0 else [wbig.ld[ib], st1[k]])
            ta = s.tok("tensor")
            wbig.used(ib, ta)
            tg = s.op("scalar", lambda e, nn=nn: e.activation(
                g1[:, nn * 512:nn * 512 + n], ba[:, 0:n], AF.Gelu,
                scale=w["sh1"][:, nn:nn + 1]), deps=[ta, g1_war[0], st2[-1] if st2 else None])
            pool.done(sa, tg)
            st2.append(tg)
        st3 = []
        for nn in range(32):
            ib = wbig.load(ctx, "h2", wslab_3d(w["h2T"], 0, 32, nn * 128, 128))
            ba, wa, sa = pool.get()
            for k in range(32):
                s.op("tensor", lambda t, k=k: t.matmul(
                    ba[:, 0:n], wbig.kslice(ib, k), g1[:, k * 512:k * 512 + n],
                    start=(k == 0), stop=(k == 31)),
                    deps=[wbig.ld[ib], st2[k], wa, st1[nn]] if k == 0 else [wbig.ld[ib], st2[k]])
            ta = s.tok("tensor")
            wbig.used(ib, ta)
            tg = s.op("scalar", lambda e, nn=nn: e.activation(
                h0[:, nn * 512:nn * 512 + n], ba[:, 0:n], AF.Gelu,
                scale=w["sh2"][:, nn:nn + 1]), deps=[ta, st3[-1] if st3 else None])
            pool.done(sa, tg)
            st3.append(tg)
        h0_war[0] = st3[-1]
        g1_war[0] = st3[-1]
        st4 = []
        for nn in range(8):
            ib = wbig.load(ctx, "w3", wslab_3d(w["w3T"], 0, 32, nn * 128, 128))
            ba, wa, sa = pool.get()
            for k in range(32):
                s.op("tensor", lambda t, k=k: t.matmul(
                    ba[:, 0:n], wbig.kslice(ib, k), h0[:, k * 512:k * 512 + n],
                    start=(k == 0), stop=(k == 31)),
                    deps=[wbig.ld[ib], st3[k], wa] if k == 0 else [wbig.ld[ib], st3[k]])
            ta = s.tok("tensor")
            wbig.used(ib, ta)
            xc = x_ap_fn(nn, t0, n)
            tr = s.op("vector", lambda v, nn=nn, xc=xc: v.scalar_tensor_tensor(
                xc, ba[:, 0:n], w["s3"][:, nn:nn + 1], xc, ALU.mult, ALU.add),
                deps=[ta, st4[-1] if st4 else None])
            pool.done(sa, tr)
            st4.append(tr)
        xb_war[0] = st1[-1]
        res_toks.extend(st4)
    return res_toks


def stream_proj(ctx, pool, wbuf, w_dram, nk, nlist, rhs_fn, evac_fn, q,
                rhs_dep_fn=None, n=512, ncols=128, bias_row=None,
                ones_row=None, bias_deps=(), kparts=128):
    s = ctx.s
    toks = []
    for i, nn in enumerate(nlist):
        ib = wbuf.load(ctx, q, wslab_3d(w_dram, 0, nk, nn * ncols, ncols, p=kparts))
        ba, wa, sa = pool.get()
        for k in range(nk):
            deps = [wbuf.ld[ib]]
            if rhs_dep_fn is not None:
                deps += rhs_dep_fn(k)
            if k == 0:
                deps.append(wa)
            s.op("tensor", lambda t, k=k: t.matmul(
                ba[0:ncols, 0:n], wbuf.kslice(ib, k), rhs_fn(k),
                start=(k == 0), stop=(k == nk - 1 and bias_row is None)),
                deps=deps)
        if bias_row is not None:
            s.op("tensor", lambda t, nn=nn: t.matmul(
                ba[0:ncols, 0:n], bias_row[0:1, nn * ncols:nn * ncols + ncols],
                ones_row[0:1, 0:n], start=False, stop=True),
                deps=list(bias_deps) + ctx.const_toks)
        ta = s.tok("tensor")
        wbuf.used(ib, ta)
        tok = evac_fn(i, nn, ba[0:ncols, 0:n], ta)
        pool.done(sa, tok)
        toks.append(tok)
    return toks


def emit_l0attnb(ctx, x_ap_fn, w, pool, scr, bufs, deps=()):
    s = ctx.s
    xb, g, sq, tmp, cbc = bufs["xb"], bufs["g"], bufs["sq"], bufs["tmp"], bufs["cbc"]
    rrow = bufs["rrow"]
    res_toks = []
    xb_war = [bufs.get("xb_war")]
    g_war = [None]
    for it in range(ctx.NT):
        t0 = it * 512
        n = 512
        ln_toks, _ = emit_ln_std(
            ctx, t0, n, lambda c: x_ap_fn(c, t0, n),
            lambda c: xb[:, c * 512:c * 512 + n],
            pool, scr, tmp, deps=list(deps) + [xb_war[0]])
        gel = [None]

        def evac_g(i, nn, bank, ta):
            gel[0] = s.op("scalar", lambda e: e.activation(
                g[:, nn * 512:nn * 512 + n], bank, AF.Gelu),
                deps=[ta, gel[0], g_war[0]])
            return gel[0]
        st1 = stream_proj(ctx, pool, bufs["wk"], w["KpT"], 8, range(32),
                          lambda k: xb[:, k * 512:k * 512 + n], evac_g, "kp",
                          rhs_dep_fn=lambda k: [ln_toks[k]], n=n)
        bank_q, war_q, sl_q = pool.get()
        sq_tok = None
        last_mm = None
        for c in range(32):
            sq_tok = s.op("vector", lambda v, c=c: v.tensor_tensor(
                sq[:, 0:n], g[:, c * 512:c * 512 + n], g[:, c * 512:c * 512 + n],
                ALU.mult), deps=[st1[c], sq_tok, last_mm])
            last_mm = s.op("tensor", lambda t, c=c: t.matmul(
                bank_q[0:1, 0:n], ctx.ones_col[:], sq[:, 0:n],
                start=(c == 0), stop=(c == 31)),
                deps=[sq_tok] + ([war_q] if c == 0 else []) + ctx.const_toks)
        t1 = s.op("scalar", lambda e: e.activation(
            rrow[0:1, 0:n], bank_q[0:1, 0:n], AF.Sqrt), deps=[last_mm])
        pool.done(sl_q, t1)
        t2 = s.op("vector", lambda v: v.reciprocal(rrow[0:1, 0:n], rrow[0:1, 0:n]),
                  deps=[t1])
        t3 = s.op("vector", lambda v: v.tensor_scalar(
            rrow[0:1, 0:n], rrow[0:1, 0:n], 64.0, None, ALU.mult), deps=[t2])
        bank_c, war_c, sl_c = pool.get()
        t4 = s.op("tensor", lambda t: t.matmul(
            bank_c[:, 0:n], ctx.ones_row[0:1, 0:128], rrow[0:1, 0:n],
            start=True, stop=True), deps=[t3, war_c] + ctx.const_toks)
        t5 = s.op("vector", lambda v: v.tensor_copy(cbc[:, 0:n], bank_c[:, 0:n]),
                  deps=[t4, res_toks[-1] if res_toks else None])
        pool.done(sl_c, t5)
        ev_last = [None]

        def evac_o(i, nn, bank, ta):
            xc = x_ap_fn(nn, t0, n)
            tt1 = s.op("vector", lambda v: v.tensor_tensor(
                tmp[:, 0:n], bank, cbc[:, 0:n], ALU.mult),
                deps=[ta, t5, ev_last[0]])
            tt2 = s.op("vector", lambda v, xc=xc: v.tensor_add(xc, xc, tmp[:, 0:n]),
                       deps=[tt1])
            ev_last[0] = tt2
            return tt2
        st2 = stream_proj(ctx, pool, bufs["wv"], w["Vp"], 32, range(8),
                          lambda k: g[:, k * 512:k * 512 + n], evac_o, "vp",
                          rhs_dep_fn=lambda k: [st1[k]], n=n)
        xb_war[0] = st1[-1]
        g_war[0] = st2[-1]
        res_toks.extend(st2)
    return res_toks


def bcast_row(ctx, pool, row_ap, dst_ap, deps, nparts=128, n=512):
    s = ctx.s
    bank, war, sl = pool.get()
    t1 = s.op("tensor", lambda t: t.matmul(
        bank[0:nparts, 0:n], ctx.ones_row[0:1, 0:nparts], row_ap,
        start=True, stop=True), deps=list(deps) + [war] + ctx.const_toks)
    t2 = s.op("vector", lambda v: v.tensor_copy(dst_ap, bank[0:nparts, 0:n]),
              deps=[t1])
    pool.done(sl, t2)
    return t2


def emit_l1attnb(ctx, x_ap_fn, w, pool, scr, bufs, carry_fn, deps=(),
                 stage="all", totals_out=None):
    s = ctx.s
    T = ctx.T
    xb, qb, kb, iqb, ikb = bufs["xb"], bufs["qb"], bufs["kb"], bufs["iqb"], bufs["ikb"]
    Fa, Fb, CFa, CFb = bufs["Fa"], bufs["Fb"], bufs["CFa"], bufs["CFb"]
    Ga, Gb, RH, ybuf = bufs["Ga"], bufs["Gb"], bufs["RH"], bufs["ybuf"]
    tmp, t2b, cbc = bufs["tmp"], bufs["t2"], bufs["cbc"]
    n = 512
    if stage == "rest":
        feat_toks = list(deps)
    else:
        z0 = s.op("vector", lambda v: v.memset(Gb[:], 0.0), deps=list(deps))
        z1 = s.op("vector", lambda v: v.memset(Fa[:], 0.0), deps=list(deps))
        z2 = s.op("vector", lambda v: v.memset(Fb[:], 0.0), deps=list(deps))
        feat_toks = [z0, z1, z2]
    war = {"xb": None}
    for it in (range(ctx.NT) if stage != "rest" else []):
        t0 = it * 512
        tile_bar = s.tok("tensor")
        ln_toks, _ = emit_ln_std(
            ctx, t0, n, lambda c: x_ap_fn(c, t0, n),
            lambda c: xb[:, c * 512:c * 512 + n],
            pool, scr, tmp, deps=list(deps) + [war["xb"], tile_bar])

        def cast_evac(dst):
            def f(i, nn, bank, ta):
                return s.op("vector", lambda v: v.tensor_copy(
                    dst[:, nn * 512:nn * 512 + n], bank), deps=[ta, tile_bar])
            return f
        tq = stream_proj(ctx, pool, bufs["wq"], w["WQT"], 8, range(8),
                         lambda k: xb[:, k * 512:k * 512 + n], cast_evac(qb), "wq",
                         rhs_dep_fn=lambda k: [ln_toks[k]], n=n)
        tk = stream_proj(ctx, pool, bufs["wk"], w["WKT"], 8, range(8),
                         lambda k: xb[:, k * 512:k * 512 + n], cast_evac(kb), "wk",
                         rhs_dep_fn=lambda k: [ln_toks[k]], n=n)
        tiq = stream_proj(ctx, pool, bufs["wi"], w["WinvT"], 8, range(8),
                          lambda k: qb[:, k * 512:k * 512 + n], cast_evac(iqb), "wi",
                          rhs_dep_fn=lambda k: [tq[k]], n=n)
        tik = stream_proj(ctx, pool, bufs["wi"], w["WinvT"], 8, range(8),
                          lambda k: kb[:, k * 512:k * 512 + n], cast_evac(ikb), "wi",
                          rhs_dep_fn=lambda k: [tk[k]], n=n)
        feats = [
            (kb, tk, "Wb", Fa, 0, z1),
            (kb, tk, "Wt", Fa, 64, z1),
            (ikb, tik, "Vb", Fb, 0, z2),
            (xb, ln_toks, "Xt", Fb, 64, z2),
            (qb, tq, "Vb", Ga, 0, None),
            (qb, tq, "Vt", Ga, 64, None),
            (iqb, tiq, "Wb", Gb, 0, z0),
        ]
        for src, stoks, wn, dst, r0, zdep in feats:
            def evac(i, nn, bank, ta, dst=dst, r0=r0, zdep=zdep):
                return s.op("vector", lambda v: v.tensor_copy(
                    dst[r0:r0 + 64, t0:t0 + n], bank[0:64, 0:n]),
                    deps=[ta, zdep])
            feat_toks += stream_proj(
                ctx, pool, bufs["wf"], w[wn], 8, [0],
                lambda k, src=src: src[:, k * 512:k * 512 + n], evac, "wf",
                rhs_dep_fn=lambda k, stoks=stoks: [stoks[k]], n=n, ncols=64)
        war["xb"] = s.tok("tensor")
    if stage == "feat":
        tt_a = s.op("vector", lambda v: v.tensor_reduce(
            bufs["tots"][:, 0:1], Fa[:, 0:ctx.T], AX.X, ALU.add), deps=feat_toks)
        tt_b = s.op("vector", lambda v: v.tensor_reduce(
            bufs["tots"][:, 1:2], Fb[:, 0:ctx.T], AX.X, ALU.add), deps=feat_toks)
        outs = []
        for nm, buf, dep in (("Fa_d", Fa, feat_toks), ("Fb_d", Fb, feat_toks),
                             ("Ga_d", Ga, feat_toks), ("Gb_d", Gb, feat_toks),
                             ("tots_d", bufs["tots"], [tt_a, tt_b])):
            outs.append(s.dma("gpsimd", "s6o", w[nm], buf[:], deps=dep))
        return outs
    ca, cda = carry_fn("a")
    cb_, cdb = carry_fn("b")
    zbc = bufs["zbc"][:, 0:1].broadcast_to([128, T])
    sc_a = s.op("vector", lambda v: v.tensor_tensor_scan(
        CFa[:, 0:T], Fa[:, 0:T], zbc, ca, ALU.add, ALU.add),
        deps=feat_toks + list(cda) + [bufs["zbc_tok"]])
    sc_b = s.op("vector", lambda v: v.tensor_tensor_scan(
        CFb[:, 0:T], Fb[:, 0:T], zbc, cb_, ALU.add, ALU.add),
        deps=feat_toks + list(cdb) + [bufs["zbc_tok"]])
    res_toks = []
    prev = [sc_a, sc_b]
    for it in range(ctx.NT):
        t0 = it * 512
        r2 = s.op("vector", lambda v: v.tensor_scalar(
            bufs["rowtmp"][0:1, 0:n], w["cinv2_row"][0:1, t0:t0 + n],
            w["at11"][0:1, 0:1], None, ALU.mult),
            deps=list(deps) + [bufs["rows_tok"], prev[-1]])
        b1 = bcast_row(ctx, pool, w["cinv_row"][0:1, t0:t0 + n], cbc[:, 0:n],
                       [bufs["rows_tok"]] + prev)
        b2 = bcast_row(ctx, pool, bufs["rowtmp"][0:1, 0:n], cbc[:, n:2 * n], [r2])
        tt1 = s.op("vector", lambda v: v.tensor_tensor(
            tmp[:, 0:n], Ga[:, t0:t0 + n], CFa[:, t0:t0 + n], ALU.mult),
            deps=[sc_a] + prev)
        tt2 = s.op("vector", lambda v: v.tensor_tensor(
            t2b[:, 0:n], Gb[:, t0:t0 + n], CFb[:, t0:t0 + n], ALU.mult),
            deps=[sc_b] + prev)
        c1 = s.op("vector", lambda v: v.tensor_scalar(
            t2b[0:64, 0:n], t2b[0:64, 0:n], bufs["ab64"][0:64, 0:1], None,
            ALU.mult), deps=[tt2, bufs["ab64_tok"]])
        c2 = s.op("vector", lambda v: v.tensor_add(
            tmp[0:64, 0:n], tmp[0:64, 0:n], t2b[0:64, 0:n]), deps=[tt1, c1])
        rh0 = s.op("vector", lambda v: v.tensor_tensor(
            RH[0:64, it * 512:it * 512 + n], tmp[0:64, 0:n], cbc[0:64, 0:n],
            ALU.mult), deps=[c2, b1])
        c3 = s.op("vector", lambda v: v.tensor_tensor(
            tmp[64:128, 0:n], tmp[64:128, 0:n], CFb[64:128, t0:t0 + n],
            ALU.mult), deps=[tt1, sc_b])
        rh1 = s.op("vector", lambda v: v.tensor_tensor(
            RH[64:128, it * 512:it * 512 + n], tmp[64:128, 0:n],
            cbc[64:128, n:2 * n], ALU.mult), deps=[c3, b2])
        ev_toks = []
        for nn in range(8):
            ba, wa, sa = pool.get()
            s.op("tensor", lambda t, nn=nn: t.matmul(
                ba[:, 0:n], bufs["ucat_sb"][:, nn * 128:(nn + 1) * 128],
                RH[:, it * 512:it * 512 + n], start=True, stop=False),
                deps=[rh0, rh1, wa, bufs["ucat_tok"]])
            m2 = s.op("tensor", lambda t, nn=nn: t.matmul(
                ba[:, 0:n], bufs["cbrow"][0:1, nn * 128:(nn + 1) * 128],
                ctx.ones_row[0:1, 0:n], start=False, stop=True),
                deps=[bufs["cb_tok"]] + ctx.const_toks)
            ev = s.op("vector", lambda v, nn=nn: v.tensor_copy(
                ybuf[:, nn * 512:nn * 512 + n], ba[:, 0:n]),
                deps=[m2, prev[0]])
            pool.done(sa, ev)
            ev_toks.append(ev)

        def evac_res(i, nn, bank, ta):
            xc = x_ap_fn(nn, t0, n)
            return s.op("vector", lambda v, xc=xc: v.tensor_add(xc, xc, bank),
                        deps=[ta, res_toks[-1] if res_toks else None])
        rt = stream_proj(ctx, pool, bufs["wq"], w["WOT"], 8, range(8),
                         lambda k: ybuf[:, k * 512:k * 512 + n], evac_res,
                         "wo", rhs_dep_fn=lambda k: [ev_toks[k]], n=n)
        res_toks += rt
        prev = [rt[-1]]
    return res_toks


def emit_l1attna(ctx, x_ap_fn, w, pool, scr, bufs, deps=()):
    s = ctx.s
    T = ctx.T
    nj = (T + 256) // 128
    xb, qs, kbuf, vbuf, att = bufs["xb"], bufs["qs"], bufs["kbuf"], bufs["vbuf"], bufs["att"]
    tmp = bufs["tmp"]
    wt = list(w["w_toks"])
    TK = T + 256
    vv = vbuf[:].rearrange("p (j c) -> p j c", c=65)
    t_ones = s.op("vector", lambda v: v.memset(vv[:, :, 64:65], 1.0), deps=list(deps))
    prod_toks = []
    war = {"xb": None, "qs": [[], []], "att": None}
    res_toks = []

    def produce(it):
        halo = it == ctx.NT
        t0 = it * 512
        nn_ = 256 if halo else 512
        if halo:
            xf = lambda c: w["halo_ap_fn"](c)
        else:
            xf = lambda c: x_ap_fn(c, t0, nn_)
        tile_bar = s.tok("tensor")
        ln_toks, _ = emit_ln_std(
            ctx, t0, nn_, xf, lambda c: xb[:, c * 512:c * 512 + nn_],
            pool, scr, tmp, deps=list(deps) + [war["xb"], tile_bar])
        if not halo:
            par = it % 2
            def evq(i, nnn, bank, ta):
                return s.op("vector", lambda v, nnn=nnn: v.tensor_scalar(
                    qs[0:64, par * 16 * 512 + nnn * 512:par * 16 * 512 + nnn * 512 + nn_],
                    bank, w["bq_cols"][0:64, nnn:nnn + 1], None, ALU.add),
                    deps=[ta] + wt + war["qs"][par])
            qtk = stream_proj(ctx, pool, bufs["wq"], w["WqT"], 8, range(16),
                              lambda k: xb[:, k * 512:k * 512 + nn_], evq,
                              "q4", rhs_dep_fn=lambda k: [ln_toks[k]], n=nn_,
                              ncols=64)
            war["qs"][par] = qtk
            prod_toks.extend(qtk)
        koff = (128 + t0) if not halo else None
        war_k = [prod_toks[-1] if prod_toks else None]

        def evk(i, g, bank, ta):
            base = g * TK
            tk1 = s.op("vector", lambda v, g=g: v.tensor_scalar(
                tmp[0:64, 0:nn_], bank, w["bk_col"][0:64, g:g + 1], None, ALU.add),
                deps=[ta, war_k[0]] + wt)
            if halo:
                tk2 = s.op("vector", lambda v: v.tensor_tensor(
                    tmp[0:64, 0:nn_], tmp[0:64, 0:nn_], bufs["hmask_bc"][0:64, 0:nn_],
                    ALU.mult), deps=[tk1, bufs["hmask_bc_tok"]])
                tk3 = s.op("vector", lambda v, base=base: v.tensor_copy(
                    kbuf[0:64, base:base + 128], tmp[0:64, 0:128]), deps=[tk2])
                war_k[0] = s.op("vector", lambda v, base=base: v.tensor_copy(
                    kbuf[0:64, base + 128 + T:base + 256 + T], tmp[0:64, 128:256]),
                    deps=[tk3])
            else:
                war_k[0] = s.op("vector", lambda v, base=base: v.tensor_copy(
                    kbuf[0:64, base + koff:base + koff + nn_], tmp[0:64, 0:nn_]),
                    deps=[tk1])
            return war_k[0]
        prod_toks.extend(stream_proj(ctx, pool, bufs["wkv"], w["WkT"], 8, [0, 1],
                         lambda k: xb[:, k * 512:k * 512 + nn_], evk,
                         "k4", rhs_dep_fn=lambda k: [ln_toks[k]], n=nn_,
                         ncols=64))
        nsub = nn_ // 128
        for st_ in range(nsub):
            ib = bufs["wv"].load(ctx, "v4", wslab_3d(w["WvT"], 0, 8, 0, 128))
            ba, wa, sa = pool.get()
            for k in range(8):
                s.op("tensor", lambda t, k=k, st_=st_: t.matmul(
                    ba[:, 0:128], xb[:, k * 512 + st_ * 128:k * 512 + st_ * 128 + 128],
                    bufs["wv"].kslice(ib, k), start=(k == 0), stop=False),
                    deps=[bufs["wv"].ld[ib], ln_toks[k]] + ([wa] if k == 0 else []))
            mb = s.op("tensor", lambda t: t.matmul(
                ba[:, 0:128], ctx.ones_row[0:1, 0:128], w["bv_row"][0:1, 0:128],
                start=False, stop=True), deps=wt + ctx.const_toks)
            bufs["wv"].used(ib, mb)
            if halo:
                tv = s.op("vector", lambda v, st_=st_: v.tensor_scalar(
                    tmp[:, 0:128], ba[:, 0:128],
                    w["hmask_col"][:, st_:st_ + 1], None, ALU.mult),
                    deps=[mb] + wt + [prod_toks[-1] if prod_toks else None])
                jj = 0 if st_ == 0 else nj - 1
            else:
                tv = s.op("vector", lambda v: v.tensor_copy(
                    tmp[:, 0:128], ba[:, 0:128]),
                    deps=[mb, prod_toks[-1] if prod_toks else None])
                jj = 1 + (t0 // 128) + st_
            for g in range(2):
                j2 = jj * 2 + g
                tvg = s.op("vector", lambda v, j2=j2, g=g: v.tensor_copy(
                    vv[:, j2:j2 + 1, 0:64].rearrange("p a b -> p (a b)"),
                    tmp[:, g * 64:g * 64 + 64]), deps=[tv, t_ones])
                prod_toks.append(tvg)
            pool.done(sa, prod_toks[-1])
        war["xb"] = s.tok("tensor")

    def attend(it):
        par = it % 2
        it0 = it * 512
        att_toks = []
        pt = bufs["pt"]
        pbar = prod_toks[-1:]
        for g in range(2):
            for p0 in range(2):
                for qt4 in range(4):
                    tq = it0 + qt4 * 128
                    h0_ = 8 * g + 4 * p0
                    rq = qs[0:64, par * 16 * 512:(par + 1) * 16 * 512].rearrange(
                        "p (c t) -> p c t", t=512)[:, h0_:h0_ + 4,
                                                   qt4 * 128:qt4 * 128 + 128]
                    bo, wo_, so = pool.get()
                    exp_toks = []
                    for jt in range(3):
                        bs, ws_, ss = pool.get()
                        m1 = s.op("tensor", lambda t, jt=jt: t.matmul(
                            bs[:, 0:512],
                            kbuf[0:64, g * TK + tq + 128 * jt:g * TK + tq + 128 * jt + 128],
                            rq, start=True, stop=True),
                            deps=pbar + war["qs"][par] + [ws_])
                        m1 = s.op("vector", lambda v: v.tensor_scalar(
                            bs[:, 0:512], bs[:, 0:512], 75.0, -60.0, ALU.min,
                            ALU.max), deps=[m1])
                        e1 = s.op("scalar", lambda e, jt=jt: e.activation(
                            pt[:, jt * 512:jt * 512 + 512], bs[:, 0:512], AF.Exp),
                            deps=[m1, exp_toks[-1] if exp_toks else war["att"]])
                        pool.done(ss, e1)
                        if jt == 0:
                            e1 = s.op("vector", lambda v, jt=jt: v.tensor_tensor(
                                pt[:, jt * 512:jt * 512 + 512].rearrange(
                                    "p (h m) -> p h m", m=128),
                                pt[:, jt * 512:jt * 512 + 512].rearrange(
                                    "p (h m) -> p h m", m=128),
                                w["mask0"][:].rearrange(
                                    "p (a m) -> p a m", a=1).broadcast_to([128, 4, 128]),
                                ALU.mult), deps=[e1] + wt)
                        elif jt == 2:
                            e1 = s.op("vector", lambda v, jt=jt: v.tensor_tensor(
                                pt[:, jt * 512:jt * 512 + 512].rearrange(
                                    "p (h m) -> p h m", m=128),
                                pt[:, jt * 512:jt * 512 + 512].rearrange(
                                    "p (h m) -> p h m", m=128),
                                w["mask2"][:].rearrange(
                                    "p (a m) -> p a m", a=1).broadcast_to([128, 4, 128]),
                                ALU.mult), deps=[e1] + wt)
                        exp_toks.append(e1)
                    for jt in range(3):
                        j2 = ((tq - 128) // 128 + 1 + jt - 1) * 2 + g
                        j2 = (tq // 128 + jt) * 2 + g
                        s.op("tensor", lambda t, jt=jt, j2=j2: t.matmul(
                            bo[0:65, 0:512],
                            vv[:, j2:j2 + 1, :].rearrange("p a b -> p (a b)"),
                            pt[:, jt * 512:jt * 512 + 512],
                            start=(jt == 0), stop=(jt == 2)),
                            deps=[exp_toks[jt]] + ([wo_] if jt == 0 else []))
                    pv = s.tok("tensor")
                    r1 = s.op("vector", lambda v: v.reciprocal(
                        bufs["linv"][0:1, 0:512], bo[64:65, 0:512]), deps=[pv])
                    bb, wb_, sb_ = pool.get()
                    m2 = s.op("tensor", lambda t: t.matmul(
                        bb[0:64, 0:512], ctx.ones_row[0:1, 0:64],
                        bufs["linv"][0:1, 0:512], start=True, stop=True),
                        deps=[r1, wb_] + ctx.const_toks)
                    c1 = s.op("vector", lambda v: v.tensor_copy(
                        bufs["lbc"][0:64, 0:512], bb[0:64, 0:512]), deps=[m2])
                    pool.done(sb_, c1)
                    adst = att[0:64, :].rearrange(
                        "p (c t) -> p c t", t=512)[:, h0_:h0_ + 4,
                                                   qt4 * 128:qt4 * 128 + 128]
                    a1 = s.op("vector", lambda v: v.tensor_tensor(
                        adst, bo[0:64, 0:512].rearrange("p (h m) -> p h m", m=128),
                        bufs["lbc"][0:64, 0:512].rearrange("p (h m) -> p h m", m=128),
                        ALU.mult), deps=[c1, war["att"]])
                    pool.done(so, a1)
                    att_toks.append(a1)
        def evr(i, nnn, bank, ta):
            xc = x_ap_fn(nnn, it0, 512)
            return s.op("vector", lambda v, xc=xc: v.tensor_add(xc, xc, bank),
                        deps=[ta, res_toks[-1] if res_toks else None])

        def rhs(k):
            return att[0:64, k * 512:(k + 1) * 512]
        res_toks.extend(stream_proj(
            ctx, pool, bufs["wo"], w["WoT"], 16, range(8), rhs,
            evr, "wo4", rhs_dep_fn=lambda k: att_toks[-1:],
            n=512, bias_row=w["bo_row"], ones_row=ctx.ones_row, bias_deps=wt,
            kparts=64))
        war["att"] = s.tok("tensor")
        war["qs"][par] = [war["att"]]

    produce(ctx.NT)
    for it in range(ctx.NT):
        produce(it)
        if it > 0:
            attend(it - 1)
    attend(ctx.NT - 1)
    return res_toks


def emit_inner_attn(ctx, z_ap_fn, zh_ap_fn, zlast_ap_fn, zlast_save_fn,
                    w, pool, scr, bufs, deps=()):
    s = ctx.s
    T = ctx.T
    hsh, q5, k5, vv5 = bufs["hsh"], bufs["q5"], bufs["k5"], bufs["vv5"]
    att5, tmp = bufs["att5"], bufs["tmp"]
    qn, qa, qb, pt5 = bufs["qn"], bufs["qa"], bufs["qb"], bufs["pt5"]
    sqh = bufs["sqh"]
    wt = list(w["w_toks"])
    vv = vv5[:].rearrange("p (j c) -> p j c", c=65)
    t_ones = s.op("vector", lambda v: v.memset(vv[:, :, 64:65], 1.0), deps=list(deps))
    tq1 = s.op("vector", lambda v: v.memset(q5[64:65, :], 1.0), deps=list(deps))
    kv0 = s.op("vector", lambda v: v.memset(k5[64:65, :], 0.0), deps=list(deps))
    war = {"hsh": None, "q5": [tq1], "k5own": [kv0], "k5prev": [kv0],
           "vvown": [t_ones], "vvprev": [t_ones], "att5": None,
           "sq": None, "qn": None, "qa": None, "qb": None}
    res_toks = []

    def proc_head(hh, which, t0, halo, sh_toks, n=512):
        col0 = (0 if which == "q" else 512) + hh * 64
        ib = bufs["wqkv5"].load(ctx, "qkv5", wslab_3d(w["wqkv"], 0, 8, col0, 64))
        ba, wa, sa = pool.get()
        for k in range(8):
            s.op("tensor", lambda t, k=k: t.matmul(
                ba[0:64, 0:n], bufs["wqkv5"].kslice(ib, k),
                hsh[:, k * 512:k * 512 + n],
                start=(k == 0), stop=(k == 7)),
                deps=[bufs["wqkv5"].ld[ib], sh_toks[k]] + ([wa] if k == 0 else []))
        ta = s.tok("tensor")
        bufs["wqkv5"].used(ib, ta)
        tcp = s.op("vector", lambda v: v.tensor_copy(qn[0:64, 0:n], ba[0:64, 0:n]),
                   deps=[ta, war["qn"]])
        pool.done(sa, tcp)
        tsq = s.op("scalar", lambda e: e.activation(
            sqh[0:64, 0:n], qn[0:64, 0:n], AF.Square), deps=[tcp, war["sq"]])
        bn, wn, sn = pool.get()
        tm = s.op("tensor", lambda t: t.matmul(
            bn[0:1, 0:n], ctx.ones_col[0:64, 0:1], sqh[0:64, 0:n],
            start=True, stop=True), deps=[tsq, wn] + ctx.const_toks)
        war["sq"] = tm
        nr = bufs["nrow"][0:1, 0:n]
        t1 = s.op("scalar", lambda e: e.activation(nr, bn[0:1, 0:n], AF.Sqrt),
                  deps=[tm, war["qa"]])
        pool.done(sn, t1)
        t2 = s.op("vector", lambda v: v.tensor_scalar(
            nr, nr, 1e-12, None, ALU.max), deps=[t1])
        t3 = s.op("vector", lambda v: v.reciprocal(nr, nr), deps=[t2])
        if which == "q":
            t3 = s.op("vector", lambda v: v.tensor_scalar(
                nr, nr, 8.0, None, ALU.mult), deps=[t3])
        bb, wb_, sb_ = pool.get()
        srow = w["qs_row"] if which == "q" else w["ks_row"]
        tb = s.op("tensor", lambda t: t.matmul(
            bb[0:64, 0:n], srow, nr, start=True, stop=True),
            deps=[t3, wb_] + wt)
        th = s.op("vector", lambda v: v.tensor_tensor(
            qa[0:64, 0:n], qn[0:64, 0:n], bb[0:64, 0:n], ALU.mult),
            deps=[tcp, tb, war["qa"]])
        pool.done(sb_, th)
        war["qn"] = th
        br, wr_, sr_ = pool.get()
        trm = s.op("tensor", lambda t: t.matmul(
            br[0:64, 0:n], w["rpermT"], qa[0:64, 0:n], start=True, stop=True),
            deps=[th, wr_] + wt)
        if which == "q":
            A = bufs["rtab"][0:64, 0:n]
            B = bufs["rtab"][0:64, 512:512 + n]
        else:
            A = bufs["rtab"][0:64, 1024:1024 + n]
            B = bufs["rtab"][0:64, 1536:1536 + n]
        t4 = s.op("vector", lambda v: v.tensor_tensor(
            qb[0:64, 0:n], qa[0:64, 0:n], A, ALU.mult),
            deps=[th, war["qb"], bufs["rtab_tok"]])
        t5 = s.op("vector", lambda v: v.tensor_tensor(
            qa[0:64, 0:n], br[0:64, 0:n], B, ALU.mult),
            deps=[trm, t4, bufs["rtab_tok"]])
        pool.done(sr_, t5)
        war["qa"] = t5
        if which == "q":
            dst = q5[0:64, hh * 512:hh * 512 + n]
            wdeps = war["q5"]
        elif halo:
            dst = k5[0:64, hh * 1024:hh * 1024 + n]
            wdeps = war["k5prev"]
        else:
            dst = k5[0:64, hh * 1024 + 512:hh * 1024 + 512 + n]
            wdeps = war["k5own"]
        t6 = s.op("vector", lambda v: v.tensor_add(dst, qb[0:64, 0:n],
                  qa[0:64, 0:n]), deps=[t5] + wdeps)
        war["qb"] = t6
        return t6

    for it in range(ctx.NT):
        t0 = it * 512
        n = 512
        if it == 0:
            sh_toks = []
            for c in range(8):
                if c < 4:
                    tcp = s.op("vector", lambda v, c=c: v.tensor_copy(
                        hsh[:, c * 512:(c + 1) * 512], zh_ap_fn(c)[:, 4:516]),
                        deps=list(deps) + [war["hsh"]])
                else:
                    tcp = s.op("vector", lambda v, c=c: v.tensor_copy(
                        hsh[:, c * 512:(c + 1) * 512], zh_ap_fn(c)[:, 3:515]),
                        deps=list(deps) + [war["hsh"]])
                sh_toks.append(tcp)
            rt = s.dma("sync", "rt5", bufs["rtab"][0:64, :].rearrange(
                "p (a t) -> p a t", a=4), w["rot_dram"][:, 0:4, 0:512],
                deps=[bufs.get("rtab_war")])
            bufs["rtab_tok"] = rt
            hk = []
            for hh in range(8):
                hk.append(proc_head(hh, "k", 0, True, sh_toks))
                hv = s.op("vector", lambda v, hh=hh: v.tensor_copy(
                    k5[64:65, hh * 1024:hh * 1024 + 512],
                    w["hvalid_row"][0:1, 0:512]), deps=[kv0] + wt)
                hk.append(hv)
            war["k5prev"] = hk
            ibv = bufs["wv5"].load(ctx, "v5", wslab_3d(w["wqkv"], 0, 8, 1024, 512))
            for st_ in range(4):
                ba, wa, sa = pool.get()
                for k in range(8):
                    s.op("tensor", lambda t, k=k, st_=st_: t.matmul(
                        ba[:, 0:512],
                        hsh[:, k * 512 + st_ * 128:k * 512 + st_ * 128 + 128],
                        bufs["wv5"].kslice(ibv, k), start=(k == 0), stop=(k == 7)),
                        deps=[bufs["wv5"].ld[ibv], sh_toks[k]] + ([wa] if k == 0 else []))
                ta = s.tok("tensor")
                tv = None
                for h in range(8):
                    j2 = st_ * 8 + h
                    tv = s.op("vector", lambda v, j2=j2, h=h: v.tensor_copy(
                        vv[:, j2:j2 + 1, 0:64].rearrange("p a b -> p (a b)"),
                        ba[:, h * 64:h * 64 + 64]),
                        deps=[ta, t_ones] + (war["vvprev"] if h == 0 else []))
                pool.done(sa, tv)
                war["vvprev"] = [tv]
            war["hsh"] = s.tok("tensor")
        else:
            cks = []
            for hh in range(8):
                cks.append(s.op("vector", lambda v, hh=hh: v.tensor_copy(
                    k5[:, hh * 1024:hh * 1024 + 512],
                    k5[:, hh * 1024 + 512:hh * 1024 + 1024]),
                    deps=[war["att5"]] + war["k5prev"]))
            war["k5prev"] = cks
            cv = s.op("vector", lambda v: v.tensor_copy(
                vv5[:, 0:4 * 8 * 65], vv5[:, 4 * 8 * 65:8 * 8 * 65]),
                deps=[war["att5"]] + war["vvprev"])
            war["vvprev"] = [cv]
        sh_toks = []
        for c in range(8):
            dep0 = [war["hsh"]] + list(deps)
            if c < 4:
                tcp = s.op("vector", lambda v, c=c: v.tensor_copy(
                    hsh[:, c * 512:(c + 1) * 512], z_ap_fn(c, t0, n)), deps=dep0)
            else:
                t_ = s.op("vector", lambda v, c=c: v.tensor_copy(
                    hsh[:, c * 512:c * 512 + 1],
                    zlast_ap_fn(c, it) if it > 0 else zh_ap_fn(c)[:, 515:516]),
                    deps=dep0 + war.get("zl", []))
                tcp = s.op("vector", lambda v, c=c: v.tensor_copy(
                    hsh[:, c * 512 + 1:(c + 1) * 512], z_ap_fn(c, t0, 511)),
                    deps=[t_])
            sh_toks.append(tcp)
        zl_toks = [zlast_save_fn(c, t0) for c in range(4, 8)]
        war["zl"] = zl_toks
        rt = s.dma("sync", "rt5", bufs["rtab"][0:64, :].rearrange(
            "p (a t) -> p a t", a=4), w["rot_dram"][:, 0:4, 512 + t0:512 + t0 + 512],
            deps=[bufs.get("rtab_war"), war["qb"]])
        bufs["rtab_tok"] = rt
        ktoks = []
        for hh in range(8):
            ktoks.append(proc_head(hh, "q", t0, False, sh_toks))
            ktoks.append(proc_head(hh, "k", t0, False, sh_toks))
        war["q5"] = ktoks
        war["k5own"] = ktoks
        bufs["rtab_war"] = ktoks[-1]
        kz = s.op("vector", lambda v: v.memset(k5[64:65, :].rearrange(
            "p (h s) -> p h s", s=1024)[:, :, 512:1024], 0.0), deps=ktoks)
        ibv = bufs["wv5"].load(ctx, "v5", wslab_3d(w["wqkv"], 0, 8, 1024, 512))
        vtoks = []
        for st_ in range(4):
            ba, wa, sa = pool.get()
            for k in range(8):
                s.op("tensor", lambda t, k=k, st_=st_: t.matmul(
                    ba[:, 0:512],
                    hsh[:, k * 512 + st_ * 128:k * 512 + st_ * 128 + 128],
                    bufs["wv5"].kslice(ibv, k), start=(k == 0), stop=(k == 7)),
                    deps=[bufs["wv5"].ld[ibv], sh_toks[k]] + ([wa] if k == 0 else []))
            ta = s.tok("tensor")
            tv = None
            for h in range(8):
                j2 = (4 + st_) * 8 + h
                tv = s.op("vector", lambda v, j2=j2, h=h: v.tensor_copy(
                    vv[:, j2:j2 + 1, 0:64].rearrange("p a b -> p (a b)"),
                    ba[:, h * 64:h * 64 + 64]),
                    deps=[ta, t_ones] + (war["vvown"] if h == 0 else []))
            vtoks.append(tv)
            pool.done(sa, tv)
        war["vvown"] = vtoks
        war["hsh"] = s.tok("tensor")
        att_toks = []
        prod_bar = ktoks + vtoks + war["k5prev"] + war["vvprev"] + [kz, tq1]
        for hh in range(8):
            for qt in range(4):
                tq = qt * 128
                rq = q5[:, hh * 512 + tq:hh * 512 + tq + 128]
                b4, w4, s4_ = pool.get()
                b1, w1_, s1_ = pool.get()
                mm_toks = []
                for jt in range(5):
                    bank = b4 if jt < 4 else b1
                    cof = jt * 128 if jt < 4 else 0
                    m1 = s.op("tensor", lambda t, jt=jt, bank=bank, cof=cof: t.matmul(
                        bank[:, cof:cof + 128],
                        k5[:, hh * 1024 + tq + 128 * jt:hh * 1024 + tq + 128 * jt + 128],
                        rq, start=True, stop=True),
                        deps=prod_bar + ([w4] if jt == 0 else ([w1_] if jt == 4 else [])))
                    mm_toks.append(m1)
                cl1 = s.op("vector", lambda v: v.tensor_scalar(
                    b4[:, 0:512], b4[:, 0:512], 75.0, -60.0, ALU.min, ALU.max),
                    deps=mm_toks[0:4])
                cl2 = s.op("vector", lambda v: v.tensor_scalar(
                    b1[:, 0:128], b1[:, 0:128], 75.0, -60.0, ALU.min, ALU.max),
                    deps=[mm_toks[4]])
                e1 = s.op("scalar", lambda e: e.activation(
                    pt5[:, 0:512], b4[:, 0:512], AF.Exp),
                    deps=[cl1] + [att_toks[-1] if att_toks else war["att5"]])
                e2 = s.op("scalar", lambda e: e.activation(
                    pt5[:, 512:640], b1[:, 0:128], AF.Exp),
                    deps=[cl2, e1])
                pool.done(s4_, e1)
                pool.done(s1_, e2)
                mk0 = s.op("vector", lambda v: v.tensor_tensor(
                    pt5[:, 0:128], pt5[:, 0:128], w["mask0"][:, 0:128], ALU.mult),
                    deps=[e1] + wt)
                mkc = s.op("vector", lambda v: v.tensor_tensor(
                    pt5[:, 512:640], pt5[:, 512:640], w["maskc"][:, 0:128], ALU.mult),
                    deps=[e2] + wt)
                bo, wo_, so_ = pool.get()
                for jt in range(5):
                    j2 = (qt + jt) * 8 + hh
                    dep = [mk0 if jt == 0 else (mkc if jt == 4 else e1)]
                    s.op("tensor", lambda t, jt=jt, j2=j2: t.matmul(
                        bo[0:65, 0:128],
                        vv[:, j2:j2 + 1, :].rearrange("p a b -> p (a b)"),
                        pt5[:, jt * 128:jt * 128 + 128] if jt < 4 else pt5[:, 512:640],
                        start=(jt == 0), stop=(jt == 4)),
                        deps=dep + ([wo_] if jt == 0 else []))
                pv = s.tok("tensor")
                r1 = s.op("vector", lambda v: v.reciprocal(
                    bufs["linv"][0:1, 0:128], bo[64:65, 0:128]), deps=[pv])
                bb, wb_, sb_ = pool.get()
                m2 = s.op("tensor", lambda t: t.matmul(
                    bb[0:64, 0:128], ctx.ones_row[0:1, 0:64],
                    bufs["linv"][0:1, 0:128], start=True, stop=True),
                    deps=[r1, wb_] + ctx.const_toks)
                c1 = s.op("vector", lambda v: v.tensor_copy(
                    bufs["lbc"][0:64, 0:128], bb[0:64, 0:128]), deps=[m2])
                pool.done(sb_, c1)
                a1 = s.op("vector", lambda v, hh=hh, tq=tq: v.tensor_tensor(
                    att5[0:64, hh * 512 + tq:hh * 512 + tq + 128], bo[0:64, 0:128],
                    bufs["lbc"][0:64, 0:128], ALU.mult), deps=[c1])
                pool.done(so_, a1)
                att_toks.append(a1)
        war["att5"] = att_toks[-1]
        def evr(i, nnn, bank, ta):
            zc = z_ap_fn(nnn, t0, n)
            return s.op("vector", lambda v, zc=zc: v.tensor_add(zc, zc, bank),
                        deps=[ta, res_toks[-1] if res_toks else None] + zl_toks)
        res_toks += stream_proj(ctx, pool, bufs["wo5"], w["wo"], 8, range(8),
                                lambda k: att5[0:64, k * 512:k * 512 + n],
                                evr, "wo5", rhs_dep_fn=lambda k: [war["att5"]],
                                n=n, kparts=64)
    return res_toks


def emit_inner_ffn(ctx, z_ap_fn, zlast_ap_fn, zlast_save_fn, w, pool, scr,
                   bufs, deps=()):
    s = ctx.s
    T = ctx.T
    hsh2, lnb, hg, gsb, tmp = (bufs["hsh2"], bufs["lnb"], bufs["hg"],
                               bufs["gsb"], bufs["tmp"])
    wt = list(w["w_toks"])
    res_toks = []
    war = {"hsh2": None, "lnb": None, "hg": bufs.get("hg_zero_tok")}
    NCH = 22
    for it in range(ctx.NT):
        t0 = it * 512
        n = 512
        sh_toks = []
        for c in range(8):
            dep0 = [war["hsh2"]] + list(deps)
            if c < 4:
                tcp = s.op("vector", lambda v, c=c: v.tensor_copy(
                    hsh2[:, c * 512:(c + 1) * 512], z_ap_fn(c, t0, n)), deps=dep0)
            else:
                t_ = s.op("vector", lambda v, c=c: v.tensor_copy(
                    hsh2[:, c * 512:c * 512 + 1], zlast_ap_fn(c, it)),
                    deps=dep0 + war.get("zl", []))
                tcp = s.op("vector", lambda v, c=c: v.tensor_copy(
                    hsh2[:, c * 512 + 1:(c + 1) * 512], z_ap_fn(c, t0, 511)),
                    deps=[t_])
            sh_toks.append(tcp)
        zl_toks = [zlast_save_fn(c, t0) for c in range(4, 8)]
        war["zl"] = zl_toks
        ln_toks, _ = emit_ln_std(
            ctx, t0, n, lambda c: hsh2[:, c * 512:c * 512 + n],
            lambda c: lnb[:, c * 512:c * 512 + n],
            pool, scr, tmp, deps=sh_toks + [war["lnb"]])
        gb_toks = []
        for c in range(8):
            tgb = s.op("vector", lambda v, c=c: v.tensor_scalar(
                lnb[:, c * 512:c * 512 + n], lnb[:, c * 512:c * 512 + n],
                w["lg_col"][:, c:c + 1], w["lb_col"][:, c:c + 1],
                ALU.mult, ALU.add), deps=[ln_toks[c]] + wt)
            gb_toks.append(tgb)
        gel = [None]
        hg_toks = []
        for j in range(NCH):
            wcols = 128 if j < 21 else 42
            ia = bufs["w15"].load(ctx, "w1a",
                                  wslab_3d(w["w1"], 0, 8, j * 128, wcols), sub=wcols)
            ba, wa, sa = pool.get()
            for k in range(8):
                s.op("tensor", lambda t, k=k: t.matmul(
                    ba[0:wcols, 0:n], bufs["w15"].kslice(ia, k)[:, 0:wcols],
                    lnb[:, k * 512:k * 512 + n], start=(k == 0), stop=(k == 7)),
                    deps=[bufs["w15"].ld[ia], gb_toks[k]] + ([wa] if k == 0 else []))
            ta = s.tok("tensor")
            bufs["w15"].used(ia, ta)
            ig = bufs["w15"].load(ctx, "w1g",
                                  wslab_3d(w["w1"], 0, 8, 2730 + j * 128, wcols),
                                  sub=wcols)
            bg, wg, sg = pool.get()
            for k in range(8):
                s.op("tensor", lambda t, k=k: t.matmul(
                    bg[0:wcols, 0:n], bufs["w15"].kslice(ig, k)[:, 0:wcols],
                    lnb[:, k * 512:k * 512 + n], start=(k == 0), stop=(k == 7)),
                    deps=[bufs["w15"].ld[ig], gb_toks[k]] + ([wg] if k == 0 else []))
            tg_ = s.tok("tensor")
            bufs["w15"].used(ig, tg_)
            t1 = s.op("scalar", lambda e, j=j, wcols=wcols: e.activation(
                gsb[0:wcols, 0:n], bg[0:wcols, 0:n], AF.Gelu,
                bias=w["b1g_col"][0:wcols, j:j + 1]),
                deps=[tg_, gel[0], hg_toks[-1] if hg_toks else None] + wt)
            gel[0] = t1
            t2 = s.op("vector", lambda v, j=j, wcols=wcols: v.scalar_tensor_tensor(
                hg[0:wcols, j * 512:j * 512 + n], ba[0:wcols, 0:n],
                w["b1a_col"][0:wcols, j:j + 1], gsb[0:wcols, 0:n],
                ALU.add, ALU.mult), deps=[t1, ta, war["hg"]] + wt)
            pool.done(sa, t2)
            pool.done(sg, t2)
            hg_toks.append(t2)

        def evr(i, nnn, bank, ta):
            zc = z_ap_fn(nnn, t0, n)
            return s.op("vector", lambda v, zc=zc, nnn=nnn: v.scalar_tensor_tensor(
                zc, bank, w["b2_col"][:, nnn:nnn + 1], zc, ALU.add, ALU.add),
                deps=[ta, res_toks[-1] if res_toks else None] + zl_toks + wt)
        rt = stream_proj(ctx, pool, bufs["w25"], w["w2p"], NCH, range(8),
                         lambda k: hg[:, k * 512:k * 512 + n], evr, "w25",
                         rhs_dep_fn=lambda k: [hg_toks[k]], n=n)
        res_toks += rt
        war["hsh2"] = s.tok("tensor")
        war["lnb"] = s.tok("tensor")
        war["hg"] = rt[-1]
    return res_toks


def emit_s0_prod(ctx, x_ap_fn, w, pool, scr, bufs, outs, deps=()):
    s = ctx.s
    T = ctx.T
    xlnf, xlnb, kout = bufs["xlnf"], bufs["xlnb"], bufs["kout"]
    tmp = bufs["tmp"]
    wt = list(w["w_toks"])
    ldrt = s.dma("sync", "rt0", bufs["rt_sb"][:], w["RT"][:, :], deps=list(deps))
    wbk_toks = []
    for which, wbk, wTh, b64 in (("q", bufs["wbkq"], w["WqT_h"], w["b64q"]),
                                 ("k", bufs["wbkk"], w["WkT_h"], w["b64k"])):
        for dmc in range(8):
            ib = bufs["wbkld"].load(ctx, "wbk",
                                    wslab_3d(wTh, 0, 16, dmc * 128, 128, p=64))
            ba, wa, sa = pool.get()
            mts = []
            for h in range(16):
                sl = bufs["wbkld"].kslice(ib, h)[0:64, :]
                rt2 = bufs["rt_sb"][0:64, 0:8]
                mts.append(s.op("tensor", lambda t, h=h, sl=sl, rt2=rt2: t.matmul(
                    ba[:, h * 8:h * 8 + 8], sl, rt2, start=True, stop=True),
                    deps=[ldrt, bufs["wbkld"].ld[ib]] + wt + ([wa] if h == 0 else [])))
            tok = s.op("vector", lambda v, dmc=dmc, wbk=wbk: v.tensor_copy(
                wbk[:, dmc * 128:(dmc + 1) * 128], ba[:, 0:128]), deps=mts)
            pool.done(sa, tok)
            bufs["wbkld"].used(ib, tok)
            wbk_toks.append(tok)
        bb, wb_, sb_ = pool.get()
        bts = []
        for h in range(16):
            bts.append(s.op("tensor", lambda t, h=h, b64=b64: t.matmul(
                bb[0:1, h * 8:h * 8 + 8], b64[0:64, h:h + 1],
                bufs["rt_sb"][0:64, 0:8], start=True, stop=True),
                deps=[ldrt] + wt + ([wb_] if h == 0 else [])))
        ro = 0 if which == "q" else 128
        tokb = s.op("vector", lambda v, ro=ro: v.tensor_copy(
            bufs["bbk"][0:1, ro:ro + 128], bb[0:1, 0:128]), deps=bts)
        pool.done(sb_, tokb)
        wbk_toks.append(tokb)
    out_toks = []
    war = {"xln": None, "k": None, "v": None, "oh": None}
    for it in range(ctx.NT):
        t0 = it * 512
        n = 512
        ln_toks, _ = emit_ln_std(
            ctx, t0, n, lambda c: x_ap_fn(c, t0, n),
            lambda c: xlnf[:, c * 512:c * 512 + n],
            pool, scr, tmp, deps=list(deps) + [war["xln"]])
        cast_toks = []
        for c in range(8):
            cast_toks.append(s.op("vector", lambda v, c=c: v.tensor_copy(
                xlnb[:, c * 512:c * 512 + n], xlnf[:, c * 512:c * 512 + n]),
                deps=[ln_toks[c], war["xln"]]))
        for which, wd, bcol, od, scl in (("k", w["Wk"], w["bk_cols"], outs["K"], None),
                                         ("q", w["Wq"], w["bq_cols"], outs["Q"], 0.125)):
            def ev(i, nn_, bank, ta, bcol=bcol, scl=scl):
                if scl is None:
                    return s.op("vector", lambda v, nn_=nn_: v.tensor_scalar(
                        kout[:, nn_ * 512:nn_ * 512 + n], bank,
                        bcol[:, nn_:nn_ + 1], None, ALU.add), deps=[ta, war["k"]] + wt)
                return s.op("vector", lambda v, nn_=nn_: v.tensor_scalar(
                    kout[:, nn_ * 512:nn_ * 512 + n], bank,
                    bcol[:, nn_:nn_ + 1], scl, ALU.add, ALU.mult),
                    deps=[ta, war["k"]] + wt)
            ktoks = stream_proj(ctx, pool, bufs["wq0"], wd, 8, range(8),
                                lambda k: xlnb[:, k * 512:k * 512 + n], ev, "s0" + which,
                                rhs_dep_fn=lambda k: [cast_toks[k]], n=n)
            st = s.dma("gpsimd", "s0o", od[:, t0:t0 + n].rearrange(
                "(c p) t -> p c t", p=128),
                kout[:].rearrange("p (c t) -> p c t", t=512)[:, :, 0:n], deps=ktoks)
            war["k"] = st
            out_toks.append(st)
        vts = []
        for st_ in range(4):
            for nn_ in range(2):
                ib = bufs["wv0"].load(ctx, "s0v", wslab_3d(w["Wv"], 0, 8, nn_ * 512, 512))
                ba, wa, sa = pool.get()
                for k in range(8):
                    s.op("tensor", lambda t, k=k, st_=st_: t.matmul(
                        ba[:, 0:512],
                        xlnb[:, k * 512 + st_ * 128:k * 512 + st_ * 128 + 128],
                        bufs["wv0"].kslice(ib, k), start=(k == 0), stop=False),
                        deps=[bufs["wv0"].ld[ib], cast_toks[k]] + ([wa] if k == 0 else []))
                mb = s.op("tensor", lambda t, nn_=nn_: t.matmul(
                    ba[:, 0:512], ctx.ones_row[0:1, 0:128],
                    w["bv_row"][0:1, nn_ * 512:nn_ * 512 + 512],
                    start=False, stop=True), deps=wt + ctx.const_toks)
                bufs["wv0"].used(ib, mb)
                tv = s.op("vector", lambda v, nn_=nn_: v.tensor_copy(
                    bufs["vout"][:, nn_ * 512:nn_ * 512 + 512], ba[:, 0:512]),
                    deps=[mb, war["v"]])
                pool.done(sa, tv)
                vts.append(tv)
            stv = s.dma("gpsimd", "s0vo", outs["V"][t0 + st_ * 128:t0 + st_ * 128 + 128, :],
                        bufs["vout"][:, 0:1024], deps=vts[-2:])
            war["v"] = stv
            out_toks.append(stv)
        for which, wbk, ro, od in (("q", bufs["wbkq"], 0, outs["ohQ"]),
                                   ("k", bufs["wbkk"], 128, outs["ohK"])):
            bp, wp, sp = pool.get()
            pts = []
            for k in range(8):
                pts.append(s.op("tensor", lambda t, k=k, wbk=wbk: t.matmul(
                    bp[:, 0:n], wbk[:, k * 128:(k + 1) * 128],
                    xlnf[:, k * 512:k * 512 + n], start=(k == 0), stop=False),
                    deps=[ln_toks[k]] + wbk_toks + ([wp] if k == 0 else [])))
            pb = s.op("tensor", lambda t, ro=ro: t.matmul(
                bp[:, 0:n], bufs["bbk"][0:1, ro:ro + 128],
                ctx.ones_row[0:1, 0:n], start=False, stop=True),
                deps=wbk_toks + ctx.const_toks)
            pf = s.op("vector", lambda v: v.tensor_copy(
                bufs["pf"][:, 0:n], bp[:, 0:n]), deps=[pb, war["oh"]])
            oh_toks = []
            for sub in range(4):
                bt_, wt_, st_2 = pool.get()
                tt1 = s.op("tensor", lambda t, sub=sub: t.transpose(
                    bt_[:, 0:128], bufs["pf"][:, sub * 128:sub * 128 + 128],
                    ctx.ident[:]), deps=[pf] + ctx.const_toks + [wt_])
                tc = s.op("vector", lambda v: v.tensor_copy(
                    bufs["ptm"][:, 0:128], bt_[:, 0:128]), deps=[tt1, war["oh"]])
                pool.done(st_2, tc)
                tm2 = s.op("vector", lambda v: v.tensor_reduce(
                    bufs["mx"][:, 0:16],
                    bufs["ptm"][:].rearrange("p (g e) -> p g e", e=8),
                    AX.X, ALU.max), deps=[tc])
                mxb = bufs["mx"][:, 0:16].rearrange("p (g o) -> p g o", o=1).broadcast_to([128, 16, 8])
                to = s.op("vector", lambda v, mxb=mxb: v.tensor_tensor(
                    bufs["oh1"][:].rearrange("p (g e) -> p g e", e=8),
                    bufs["ptm"][:].rearrange("p (g e) -> p g e", e=8),
                    mxb, ALU.is_equal), deps=[tm2])
                to2 = s.op("vector", lambda v: v.tensor_scalar(
                    bufs["oh1"][:, 0:128], bufs["oh1"][:, 0:128], 8.0, None,
                    ALU.mult), deps=[to])
                bt2, wt2, st_3 = pool.get()
                tt2 = s.op("tensor", lambda t: t.transpose(
                    bt2[:, 0:128], bufs["oh1"][:, 0:128], ctx.ident[:]),
                    deps=[to2] + ctx.const_toks + [wt2])
                tc2 = s.op("vector", lambda v, sub=sub: v.tensor_copy(
                    bufs["ohb"][:, sub * 128:sub * 128 + 128], bt2[:, 0:128]),
                    deps=[tt2, war["oh"]])
                pool.done(st_3, tc2)
                war["oh"] = tc2
                oh_toks.append(tc2)
            pool.done(sp, oh_toks[-1])
            sto = s.dma("gpsimd", "s0oh", od[:, t0:t0 + n],
                        bufs["ohb"][:, 0:n], deps=oh_toks)
            war["oh"] = sto
            out_toks.append(sto)
        war["xln"] = s.tok("tensor")
    return out_toks


def emit_s0_flash(ctx, x_ap_fn, w, pool, scr, bufs, deps=()):
    s = ctx.s
    T = ctx.T
    S = w["S"]
    nj = S // 128
    wt = list(w["w_toks"])
    kaug, vaug = bufs["kaug"], bufs["vaug"]
    att0, pth = bufs["att0"], bufs["pth"]
    kv_war = [[], []]
    kv_tok = [None, None]

    def load_head(h):
        b = h % 2
        t1 = s.dma("sync", f"ka{b}", kaug[b][:], w["Kaug"][h], deps=kv_war[b])
        t2 = s.dma("sync", f"va{b}", vaug[b][:], w["Vaug"][h], deps=kv_war[b])
        kv_war[b] = []
        kv_tok[b] = [t1, t2]
    load_head(0)
    res_toks = []
    att_war = [None]
    for it in range(ctx.NT):
        t0 = it * 512
        n = 512
        att_toks = []
        for h in range(16):
            b = h % 2
            ldqh = s.dma("sync", "qh", bufs["qaug"][:, 0:n],
                         w["Qaug"][h][:, t0:t0 + n],
                         deps=[bufs["q_war"]] if bufs.get("q_war") else [])
            bo = bufs["pvb"]
            wo_ = bufs.get("pvb_war")
            last_exp = [att_war[0]]
            for j in range(nj):
                bs, ws_, ss_ = pool.get()
                m1 = s.op("tensor", lambda t, j=j, b=b: t.matmul(
                    bs[:, 0:n], kaug[b][:, j * 128:(j + 1) * 128],
                    bufs["qaug"][:, 0:n], start=True, stop=True),
                    deps=kv_tok[b] + [ldqh, ws_])
                e1 = s.op("scalar", lambda e, j=j: e.activation(
                    pth[:, 0:n], bs[:, 0:n], AF.Exp, bias=bufs["nbig"]),
                    deps=[m1, last_exp[0]] + ctx.const_toks)
                pool.done(ss_, e1)
                last_exp[0] = e1
                m2 = s.op("tensor", lambda t, j=j, b=b: t.matmul(
                    bo[0:65, 0:n],
                    vaug[b][:].rearrange("p (j c) -> p j c", c=65)[
                        :, j:j + 1, :].rearrange("p a b -> p (a b)"),
                    pth[:, 0:n], start=(j == 0), stop=(j == nj - 1)),
                    deps=[e1] + ([wo_] if j == 0 else []))
            pv = s.tok("tensor")
            bufs["q_war"] = pv
            if h + 1 < 16:
                load_head(h + 1)
            elif it + 1 < ctx.NT:
                load_head(0)
            kv_war[b].append(pv)
            r1 = s.op("vector", lambda v: v.reciprocal(
                bufs["linv"][0:1, 0:n], bo[64:65, 0:n]), deps=[pv])
            bb, wb_, sb_ = pool.get()
            m3 = s.op("tensor", lambda t: t.matmul(
                bb[0:64, 0:n], ctx.ones_row[0:1, 0:64], bufs["linv"][0:1, 0:n],
                start=True, stop=True), deps=[r1, wb_] + ctx.const_toks)
            c1 = s.op("vector", lambda v: v.tensor_copy(
                bufs["lbc"][0:64, 0:n], bb[0:64, 0:n]), deps=[m3])
            pool.done(sb_, c1)
            a1 = s.op("vector", lambda v, h=h: v.tensor_tensor(
                att0[0:64, h * 512:h * 512 + n], bo[0:64, 0:n],
                bufs["lbc"][0:64, 0:n], ALU.mult),
                deps=[c1, att_war[0]])
            bufs["pvb_war"] = a1
            att_toks.append(a1)
        att_war[0] = att_toks[-1]

        def evr(i, nnn, bank, ta):
            xc = x_ap_fn(nnn, t0, n)
            return s.op("vector", lambda v, xc=xc, nnn=nnn: v.scalar_tensor_tensor(
                xc, bank, w["bo_col"][:, nnn:nnn + 1], xc, ALU.add, ALU.add),
                deps=[ta, res_toks[-1] if res_toks else None] + wt)
        res_toks += stream_proj(ctx, pool, bufs["wo0"], w["Wo"], 16, range(8),
                                lambda k: att0[0:64, k * 512:k * 512 + n],
                                evr, "s0wo", rhs_dep_fn=lambda k: [att_war[0]],
                                n=n, kparts=64)
    return res_toks


"""Orchestration: the unfused multi-launch kernel. Appended to kernel.py."""
import numpy as np
import ml_dtypes

BF = ml_dtypes.bfloat16
TT = 2048
SS = 8192
NB, NQ = 2, 4


def _corepos(c):
    return c // 4, c % 4


def _sbdefs(nc, defs):
    out = {}
    for nm, shp, dt in defs:
        out[nm] = nc.alloc_sbuf_tensor(nm, shp, dt)
    return out


def _ldx(ctx, s, xsb, xdram):
    return s.dma("sync", "x", xsb[:].rearrange("p (k t) -> p k t", k=8),
                 xdram[:, :].rearrange("(k p) t -> p k t", p=128))


def _stx(ctx, s, ydram, xsb, deps):
    st = s.dma("sync", "out", ydram[:, :].rearrange("(k p) t -> p k t", p=128),
               xsb[:].rearrange("p (k t) -> p k t", k=8), deps=deps)
    s.waitall("sync", [st])


def _x_ap(xsb):
    def f(c, t0, n):
        return xsb[:, c * TT + t0: c * TT + t0 + n]
    return f



def build_s0_prod():
    nc = bass.Bass("TRN2", target_bir_lowering=False)
    dr = {}
    for nm, shp in [("xin", [1024, TT]), ("Wq", [1024, 1024]), ("Wk", [1024, 1024]),
                    ("Wv", [1024, 1024]), ("WqT", [1024, 1024]), ("WkT", [1024, 1024]),
                    ("RT", [128, 8]), ("b64q", [64, 16]), ("b64k", [64, 16]),
                    ("bq", [128, 8]), ("bk", [128, 8]), ("bv", [1, 1024]),
                    ("ident", [128, 128])]:
        dr[nm] = nc.dram_tensor(nm, shp, F32, kind="ExternalInput").ap()
    outs = {}
    for nm, shp in [("K", [1024, TT]), ("Q", [1024, TT]), ("ohQ", [128, TT]),
                    ("ohK", [128, TT])]:
        outs[nm] = nc.dram_tensor(nm, shp, BF16, kind="ExternalOutput").ap()
    outs["V"] = nc.dram_tensor("V", [TT, 1024], BF16, kind="ExternalOutput").ap()
    ctx = Ctx(nc, TT)
    sb = _sbdefs(nc, [("x", [128, 8 * TT], F32), ("xlnf", [128, 8 * 512], F32),
                      ("xlnb", [128, 8 * 512], BF16), ("kout", [128, 8 * 512], BF16),
                      ("wbkq", [128, 8 * 128], F32), ("wbkk", [128, 8 * 128], F32),
                      ("bbk", [1, 256], F32), ("rt_sb", [128, 8], F32),
                      ("ptm", [128, 128], F32), ("mx", [128, 16], F32),
                      ("oh1", [128, 128], F32), ("ohb", [128, 512], BF16),
                      ("tmp", [128, 512], F32), ("vout", [128, 2048], BF16),
                      ("pf", [128, 512], F32),
                      ("b64q_sb", [64, 16], F32), ("b64k_sb", [64, 16], F32),
                      ("bqc", [128, 8], F32), ("bkc", [128, 8], F32),
                      ("bvr", [1, 1024], F32)])
    with nc.Block() as block:
        @block.sync
        def _(sync):
            s = ctx.s
            ctx.load_consts(ident_dram=dr["ident"][:, :])
            pool = PsumPool(nc, 6)
            scr = LnScratch(nc)
            ldx = _ldx(ctx, s, sb["x"], dr["xin"])
            wts = []
            for nm, dst in [("b64q", "b64q_sb"), ("b64k", "b64k_sb"),
                            ("bq", "bqc"), ("bk", "bkc"), ("bv", "bvr")]:
                wts.append(s.dma("sync", "cw", sb[dst][:], dr[nm][:, :]))
            w = {"Wq": dr["Wq"], "Wk": dr["Wk"], "Wv": dr["Wv"],
                 "WqT_h": dr["WqT"], "WkT_h": dr["WkT"], "RT": dr["RT"],
                 "b64q": sb["b64q_sb"], "b64k": sb["b64k_sb"],
                 "bq_cols": sb["bqc"], "bk_cols": sb["bkc"],
                 "bv_row": sb["bvr"], "w_toks": wts}
            bufs = dict(sb)
            bufs["wq0"] = WBuf(nc, "wq0", 8, 128)
            bufs["wv0"] = WBuf(nc, "wv0", 8, 512)
            bufs["wbkld"] = WBuf(nc, "wbkld", 16, 128, dtype=F32, parts=64)
            toks = emit_s0_prod(ctx, _x_ap(sb["x"]), w, pool, scr, bufs, outs,
                                deps=[ldx])
            s.waitall("gpsimd", toks)
            s.waitall("sync", toks)
    return nc


def build_s0_flash():
    nc = bass.Bass("TRN2", target_bir_lowering=False)
    nj = SS // 128
    dr = {}
    dr["Kaug"] = nc.dram_tensor("Kaug", [16, 72, SS], BF16, kind="ExternalInput").ap()
    dr["Vaug"] = nc.dram_tensor("Vaug", [16, 128, (SS // 128) * 65], BF16, kind="ExternalInput").ap()
    dr["Qaug"] = nc.dram_tensor("Qaug", [16, 72, TT], BF16, kind="ExternalInput").ap()
    dr["Wo"] = nc.dram_tensor("Wo", [1024, 1024], F32, kind="ExternalInput").ap()
    dr["bo"] = nc.dram_tensor("bo", [128, 8], F32, kind="ExternalInput").ap()
    dr["xin"] = nc.dram_tensor("xin", [1024, TT], F32, kind="ExternalInput").ap()
    yout = nc.dram_tensor("yout", [1024, TT], F32, kind="ExternalOutput").ap()
    ctx = Ctx(nc, TT)
    sb = _sbdefs(nc, [("x", [128, 8 * TT], F32),
                      ("ka0", [72, SS], BF16), ("ka1", [72, SS], BF16),
                      ("va0", [128, nj * 65], BF16), ("va1", [128, nj * 65], BF16),
                      ("qaug", [72, 512], BF16), ("pth", [128, 512], BF16),
                      ("linv", [1, 512], F32), ("lbc", [64, 512], F32),
                      ("att0", [64, 16 * 512], BF16), ("boc", [128, 8], F32)])
    with nc.Block() as block:
        @block.sync
        def _(sync):
            s = ctx.s
            ctx.load_consts()
            pool = PsumPool(nc, 5)
            scr = LnScratch(nc)
            ldx = _ldx(ctx, s, sb["x"], dr["xin"])
            wts = [s.dma("sync", "cw", sb["boc"][:], dr["bo"][:, :]), ldx]
            bufs = dict(sb)
            bufs["pvb"] = nc.alloc_psum_tensor("pvb", [128, 512], F32)
            bufs["kaug"] = [sb["ka0"], sb["ka1"]]
            bufs["vaug"] = [sb["va0"], sb["va1"]]
            bufs["wo0"] = WBuf(nc, "wo0", 16, 128, parts=64)
            bufs["nbig"] = constf(ctx, -64.0, parts=128)
            w = {"Kaug": dr["Kaug"], "Vaug": dr["Vaug"], "Qaug": dr["Qaug"],
                 "Wo": dr["Wo"], "bo_col": sb["boc"], "S": SS, "w_toks": wts}
            toks = emit_s0_flash(ctx, _x_ap(sb["x"]), w, pool, scr, bufs, deps=[ldx])
            _stx(ctx, s, yout, sb["x"], toks)
    return nc


def build_s1():
    nc = bass.Bass("TRN2", target_bir_lowering=False)
    dr = {}
    for nm, shp in [("xin", [1024, TT]), ("w1T", [1024, 4096]), ("w2T", [1024, 4096]),
                    ("h1T", [4096, 4096]), ("h2T", [4096, 4096]), ("w3T", [4096, 1024]),
                    ("scl", [128, 136])]:
        dr[nm] = nc.dram_tensor(nm, shp, F32, kind="ExternalInput").ap()
    yout = nc.dram_tensor("yout", [1024, TT], F32, kind="ExternalOutput").ap()
    ctx = Ctx(nc, TT)
    sb = _sbdefs(nc, [("x", [128, 8 * TT], F32), ("xb", [128, 8 * 512], BF16),
                      ("h0", [128, 32 * 512], BF16), ("g1", [128, 32 * 512], BF16),
                      ("asb", [128, 512], BF16), ("tmp", [128, 512], F32),
                      ("scols", [128, 136], F32)])
    with nc.Block() as block:
        @block.sync
        def _(sync):
            s = ctx.s
            ctx.load_consts()
            pool = PsumPool(nc, 6)
            scr = LnScratch(nc)
            ldx = _ldx(ctx, s, sb["x"], dr["xin"])
            lds = s.dma("sync", "scl", sb["scols"][:], dr["scl"][:, :])
            w = {"w1T": dr["w1T"], "w2T": dr["w2T"], "h1T": dr["h1T"],
                 "h2T": dr["h2T"], "w3T": dr["w3T"],
                 "s1": sb["scols"][:, 0:32], "s2": sb["scols"][:, 32:64],
                 "sh1": sb["scols"][:, 64:96], "sh2": sb["scols"][:, 96:128],
                 "s3": sb["scols"][:, 128:136]}
            bufs = {"xb": sb["xb"], "h0": sb["h0"], "g1": sb["g1"],
                    "asb": sb["asb"], "tmp": sb["tmp"],
                    "wb1": WBuf(nc, "wb1", 8, 128), "wb2": WBuf(nc, "wb2", 8, 128),
                    "wbig": WBuf(nc, "wbig", 32, 128)}
            toks = emit_l0ffn(ctx, _x_ap(sb["x"]), w, pool, scr, bufs,
                              deps=[ldx, lds])
            _stx(ctx, s, yout, sb["x"], toks)
    return nc


def build_s2():
    nc = bass.Bass("TRN2", target_bir_lowering=False)
    dr = {}
    for nm, shp in [("xin", [1024, TT]), ("KpT", [1024, 4096]), ("Vp", [4096, 1024])]:
        dr[nm] = nc.dram_tensor(nm, shp, F32, kind="ExternalInput").ap()
    yout = nc.dram_tensor("yout", [1024, TT], F32, kind="ExternalOutput").ap()
    ctx = Ctx(nc, TT)
    sb = _sbdefs(nc, [("x", [128, 8 * TT], F32), ("xb", [128, 8 * 512], BF16),
                      ("g", [128, 32 * 512], BF16), ("sq", [128, 512], F32),
                      ("tmp", [128, 512], F32), ("cbc", [128, 512], F32),
                      ("rrow", [1, 512], F32)])
    with nc.Block() as block:
        @block.sync
        def _(sync):
            s = ctx.s
            ctx.load_consts()
            pool = PsumPool(nc, 6)
            scr = LnScratch(nc)
            ldx = _ldx(ctx, s, sb["x"], dr["xin"])
            bufs = dict(sb)
            bufs["wk"] = WBuf(nc, "wk", 8, 128)
            bufs["wv"] = WBuf(nc, "wv", 32, 128)
            toks = emit_l0attnb(ctx, _x_ap(sb["x"]), {"KpT": dr["KpT"],
                                "Vp": dr["Vp"]}, pool, scr, bufs, deps=[ldx])
            _stx(ctx, s, yout, sb["x"], toks)
    return nc


def build_s4():
    nc = bass.Bass("TRN2", target_bir_lowering=False)
    dr = {}
    for nm, shp in [("xin", [1024, TT]), ("xhalo", [1024, 256]),
                    ("WqT", [1024, 1024]), ("WkT", [1024, 128]),
                    ("WvT", [1024, 128]), ("WoT", [1024, 1024]),
                    ("bq", [64, 16]), ("bk", [64, 2]), ("bv", [1, 128]),
                    ("bo", [1, 1024]), ("mask0", [128, 128]), ("mask2", [128, 128]),
                    ("hmrow", [1, 256]), ("hmcol", [128, 2])]:
        dr[nm] = nc.dram_tensor(nm, shp, F32, kind="ExternalInput").ap()
    yout = nc.dram_tensor("yout", [1024, TT], F32, kind="ExternalOutput").ap()
    ctx = Ctx(nc, TT)
    nj = (TT + 256) // 128
    sb = _sbdefs(nc, [("x", [128, 8 * TT], F32), ("xh", [128, 8 * 256], F32),
                      ("xb", [128, 8 * 512], BF16), ("qs", [64, 2 * 16 * 512], BF16),
                      ("kbuf", [64, 2 * (TT + 256)], BF16),
                      ("vbuf", [128, nj * 2 * 65], BF16),
                      ("att", [64, 16 * 512], BF16), ("pt", [128, 3 * 512], BF16),
                      ("linv", [1, 512], F32), ("lbc", [64, 512], F32),
                      ("tmp", [128, 512], F32), ("consts", [64, 18], F32),
                      ("rows", [1, 1024 + 128 + 256], F32),
                      ("m0", [128, 128], F32), ("m2", [128, 128], F32),
                      ("hmc", [128, 2], F32), ("hbc", [128, 256], F32)])
    with nc.Block() as block:
        @block.sync
        def _(sync):
            s = ctx.s
            ctx.load_consts()
            pool = PsumPool(nc, 6)
            scr = LnScratch(nc)
            ldx = _ldx(ctx, s, sb["x"], dr["xin"])
            ldh = s.dma("sync", "xh", sb["xh"][:].rearrange("p (k t) -> p k t", k=8),
                        dr["xhalo"][:, :].rearrange("(k p) t -> p k t", p=128))
            wts = [ldh]
            wts.append(s.dma("sync", "c1", sb["consts"][:, 0:16], dr["bq"][:, :]))
            wts.append(s.dma("sync", "c1", sb["consts"][:, 16:18], dr["bk"][:, :]))
            wts.append(s.dma("sync", "c2", sb["rows"][0:1, 0:1024], dr["bo"][:, :]))
            wts.append(s.dma("sync", "c2", sb["rows"][0:1, 1024:1152], dr["bv"][:, :]))
            wts.append(s.dma("sync", "c2", sb["rows"][0:1, 1152:1408], dr["hmrow"][:, :]))
            wts.append(s.dma("sync", "c3", sb["m0"][:], dr["mask0"][:, :]))
            wts.append(s.dma("sync", "c3", sb["m2"][:], dr["mask2"][:, :]))
            wts.append(s.dma("sync", "c3", sb["hmc"][:], dr["hmcol"][:, :]))
            hb = bcast_row(ctx, pool, sb["rows"][0:1, 1152:1408], sb["hbc"][:, 0:256],
                           wts, nparts=128, n=256)
            w = {"WqT": dr["WqT"], "WkT": dr["WkT"], "WvT": dr["WvT"],
                 "WoT": dr["WoT"], "bq_cols": sb["consts"][:, 0:16],
                 "bk_col": sb["consts"][:, 16:18],
                 "bv_row": sb["rows"][0:1, 1024:1152],
                 "bo_row": sb["rows"][0:1, 0:1024],
                 "mask0": sb["m0"], "mask2": sb["m2"],
                 "hmask_col": sb["hmc"], "w_toks": wts + [hb],
                 "halo_ap_fn": lambda c: sb["xh"][:, c * 256:(c + 1) * 256]}
            bufs = dict(sb)
            bufs["wq"] = WBuf(nc, "wwq", 8, 64)
            bufs["wkv"] = WBuf(nc, "wwkv", 8, 64)
            bufs["wv"] = WBuf(nc, "wwv", 8, 128)
            bufs["wo"] = WBuf(nc, "wwo", 16, 128, parts=64)
            bufs["hmask_bc"] = sb["hbc"]
            bufs["hmask_bc_tok"] = hb
            toks = emit_l1attna(ctx, _x_ap(sb["x"]), w, pool, scr, bufs,
                                deps=[ldx, ldh])
            _stx(ctx, s, yout, sb["x"], toks)
    return nc


def build_s5a(first):
    nc = bass.Bass("TRN2", target_bir_lowering=False)
    TK = 1024
    dr = {}
    for nm, shp in [("zin", [1024, TT]), ("zhalo", [1024, 516]),
                    ("wqkv", [1024, 1536]), ("wo", [512, 1024]),
                    ("rot", [64, 4, TT + 512]), ("rperm", [64, 64]),
                    ("m0", [128, 128]), ("mc", [128, 128]),
                    ("rows", [1, 2 * 64 + 512])]:
        dr[nm] = nc.dram_tensor(nm, shp, F32, kind="ExternalInput").ap()
    zout = nc.dram_tensor("zout", [1024, TT], F32, kind="ExternalOutput").ap()
    ctx = Ctx(nc, TT)
    sb = _sbdefs(nc, [
        ("z", [128, 8 * TT], F32), ("zh", [128, 8 * 516], F32),
        ("hsh", [128, 8 * 512], BF16), ("q5", [65, 8 * 512], BF16),
        ("k5", [65, 8 * 1024], BF16), ("vv5", [128, 8 * 8 * 65], BF16),
        ("sqh", [64, 512], F32), ("nrow", [1, 512], F32),
        ("qn", [64, 512], F32), ("qa", [64, 512], F32), ("qb", [64, 512], F32),
        ("pt5", [128, 640], BF16), ("linv", [1, 128], F32),
        ("lbc", [64, 128], F32), ("att5", [64, 8 * 512], BF16),
        ("tmp", [128, 512], F32), ("zl", [128, 16], F32),
        ("rtab", [64, 4 * 512], F32), ("rpermT", [64, 64], F32),
        ("m0s", [128, 128], F32), ("mcs", [128, 128], F32),
        ("rows_sb", [1, 2 * 64 + 512], F32)])
    with nc.Block() as block:
        @block.sync
        def _(sync):
            s = ctx.s
            ctx.load_consts()
            pool = PsumPool(nc, 6)
            scr = LnScratch(nc)
            wts = []
            wts.append(s.dma("sync", "c2", sb["rpermT"][:], dr["rperm"][:, :]))
            wts.append(s.dma("sync", "c2", sb["m0s"][:], dr["m0"][:, :]))
            wts.append(s.dma("sync", "c2", sb["mcs"][:], dr["mc"][:, :]))
            wts.append(s.dma("sync", "c3", sb["rows_sb"][:], dr["rows"][:, :]))
            rows = sb["rows_sb"]
            toks = []
            if first:
                xtmp = nc.alloc_sbuf_tensor("xtmp", [128, 8 * 512], F32)
                ldz = _ldx(ctx, s, sb["z"], dr["zin"])
                ldh = s.dma("sync", "xh", sb["zh"][:].rearrange("p (k t) -> p k t", k=8),
                            dr["zhalo"][:, :].rearrange("(k p) t -> p k t", p=128))
                for it in range(ctx.NT):
                    t0 = it * 512
                    cp = []
                    for c in range(8):
                        cp.append(s.op("vector", lambda v, c=c, t0=t0: v.tensor_copy(
                            xtmp[:, c * 512:(c + 1) * 512],
                            sb["z"][:, c * TT + t0:c * TT + t0 + 512]),
                            deps=[ldz, toks[-1] if toks else None]))
                    lt, _ = emit_ln_std(
                        ctx, t0, 512, lambda c: xtmp[:, c * 512:(c + 1) * 512],
                        lambda c: sb["z"][:, c * TT + t0:c * TT + t0 + 512],
                        pool, scr, sb["tmp"], deps=cp)
                    toks += lt
                for (h0, hn) in ((0, 512), (512, 4)):
                    cp = []
                    for c in range(8):
                        cp.append(s.op("vector", lambda v, c=c, h0=h0, hn=hn: v.tensor_copy(
                            xtmp[:, c * 512:c * 512 + hn],
                            sb["zh"][:, c * 516 + h0:c * 516 + h0 + hn]),
                            deps=[ldh, toks[-1] if toks else None]))
                    lt, _ = emit_ln_std(
                        ctx, 0, hn, lambda c: xtmp[:, c * 512:c * 512 + hn],
                        lambda c, h0=h0: sb["zh"][:, c * 516 + h0:c * 516 + h0 + hn],
                        pool, scr, sb["tmp"], deps=cp)
                    toks += lt
            else:
                ldz = _ldx(ctx, s, sb["z"], dr["zin"])
                ldh = s.dma("sync", "xh", sb["zh"][:].rearrange("p (k t) -> p k t", k=8),
                            dr["zhalo"][:, :].rearrange("(k p) t -> p k t", p=128))
                toks = [ldz, ldh]
            bufs = dict(sb)
            bufs["wqkv5"] = WBuf(nc, "wqkv5", 8, 64)
            bufs["wv5"] = WBuf(nc, "wv5", 8, 512)
            bufs["wo5"] = WBuf(nc, "wo5", 8, 128, parts=64)

            def z_ap(c, t0, n):
                return sb["z"][:, c * TT + t0: c * TT + t0 + n]

            def zh_ap(c):
                return sb["zh"][:, c * 516:(c + 1) * 516]

            def save(c, t0):
                it = t0 // 512
                return s.op("vector", lambda v: v.tensor_copy(
                    sb["zl"][:, c * 2 + it % 2:c * 2 + it % 2 + 1],
                    sb["z"][:, c * TT + t0 + 511:c * TT + t0 + 512]), deps=[])

            def read(c, it):
                return sb["zl"][:, c * 2 + (it - 1) % 2:c * 2 + (it - 1) % 2 + 1]
            wA = {"wqkv": dr["wqkv"], "wo": dr["wo"],
                  "qs_row": rows[0:1, 0:64], "ks_row": rows[0:1, 64:128],
                  "rot_dram": dr["rot"], "rpermT": sb["rpermT"][:],
                  "mask0": sb["m0s"], "maskc": sb["mcs"],
                  "hvalid_row": rows[0:1, 128:640],
                  "w_toks": wts + toks}
            t_at = emit_inner_attn(ctx, z_ap, zh_ap, read, save, wA, pool, scr,
                                   bufs, deps=toks + wts)
            _stx(ctx, s, zout, sb["z"], t_at)
    return nc


def build_s5b(last):
    nc = bass.Bass("TRN2", target_bir_lowering=False)
    dr = {}
    for nm, shp in [("zin", [1024, TT]), ("zm1", [128, 8]),
                    ("w1", [1024, 5460]), ("w2p", [2816, 1024]),
                    ("cols", [128, 68])]:
        dr[nm] = nc.dram_tensor(nm, shp, F32, kind="ExternalInput").ap()
    if last:
        dr["xres"] = nc.dram_tensor("xres", [1024, TT], F32, kind="ExternalInput").ap()
    zout = nc.dram_tensor("zout", [1024, TT], F32, kind="ExternalOutput").ap()
    ctx = Ctx(nc, TT)
    sb = _sbdefs(nc, [
        ("z", [128, 8 * TT], F32), ("zm", [128, 8], F32),
        ("hsh2", [128, 8 * 512], F32), ("lnb", [128, 8 * 512], BF16),
        ("hg", [128, 22 * 512], BF16), ("gsb", [128, 512], BF16),
        ("tmp", [128, 512], F32), ("cols_sb", [128, 68], F32),
        ("zl", [128, 16], F32)])
    if last:
        sb["xr"] = nc.alloc_sbuf_tensor("xr", [128, 8 * TT], F32)
    with nc.Block() as block:
        @block.sync
        def _(sync):
            s = ctx.s
            ctx.load_consts()
            pool = PsumPool(nc, 6)
            scr = LnScratch(nc)
            ldz = _ldx(ctx, s, sb["z"], dr["zin"])
            ldm = s.dma("sync", "zm", sb["zm"][:], dr["zm1"][:, :])
            ldc = s.dma("sync", "cc", sb["cols_sb"][:], dr["cols"][:, :])
            hg_zero = s.op("vector", lambda v: v.memset(sb["hg"][:], 0.0))
            toks = [ldz, ldm, ldc]
            if last:
                ldr = _ldx(ctx, s, sb["xr"], dr["xres"])
                toks.append(ldr)
            bufs = dict(sb)
            bufs["w15"] = WBuf(nc, "w15", 8, 128)
            bufs["w25"] = WBuf(nc, "w25", 22, 128)
            bufs["hg_zero_tok"] = hg_zero

            def z_ap(c, t0, n):
                return sb["z"][:, c * TT + t0: c * TT + t0 + n]

            def save(c, t0):
                it = t0 // 512
                return s.op("vector", lambda v: v.tensor_copy(
                    sb["zl"][:, c * 2 + it % 2:c * 2 + it % 2 + 1],
                    sb["z"][:, c * TT + t0 + 511:c * TT + t0 + 512]), deps=[])

            def read(c, it):
                if it == 0:
                    return sb["zm"][:, c:c + 1]
                return sb["zl"][:, c * 2 + (it - 1) % 2:c * 2 + (it - 1) % 2 + 1]
            cols = sb["cols_sb"]
            wF = {"w1": dr["w1"], "w2p": dr["w2p"],
                  "b1a_col": cols[:, 0:22], "b1g_col": cols[:, 22:44],
                  "b2_col": cols[:, 44:52], "lg_col": cols[:, 52:60],
                  "lb_col": cols[:, 60:68], "w_toks": [ldc]}
            t_ff = emit_inner_ffn(ctx, z_ap, read, save, wF, pool, scr, bufs,
                                  deps=toks)
            if last:
                add_toks = []
                for c in range(8):
                    for it in range(ctx.NT):
                        t0 = it * 512
                        add_toks.append(s.op("vector", lambda v, c=c, t0=t0: v.tensor_add(
                            sb["z"][:, c * TT + t0:c * TT + t0 + 512],
                            sb["z"][:, c * TT + t0:c * TT + t0 + 512],
                            sb["xr"][:, c * TT + t0:c * TT + t0 + 512]),
                            deps=t_ff + [ldr]))
                t_ff = add_toks
            _stx(ctx, s, zout, sb["z"], t_ff)
    return nc


def build_s6(stage):
    nc = bass.Bass("TRN2", target_bir_lowering=False)
    dr = {}
    names = [("xin", [1024, TT])]
    if stage == "feat":
        names += [("WQT", [1024, 1024]), ("WKT", [1024, 1024]),
                  ("WinvT", [1024, 1024]),
                  ("Vb", [1024, 64]), ("Wb", [1024, 64]), ("Vt", [1024, 64]),
                  ("Wt", [1024, 64]), ("Xt", [1024, 64])]
    else:
        names += [("WOT", [1024, 1024]), ("Ucat", [128, 1024]),
                  ("rows", [1, 2 * TT + 2050]), ("G8", [8, 256]), ("wsel", [8, 1])]
    for nm, shp in names:
        dr[nm] = nc.dram_tensor(nm, shp, F32, kind="ExternalInput").ap()
    if stage == "feat":
        for nm, shp in [("Fa_d", [128, TT]), ("Fb_d", [128, TT]),
                        ("Ga_d", [128, TT]), ("Gb_d", [128, TT]),
                        ("tots_d", [128, 2])]:
            dr[nm] = nc.dram_tensor(nm, shp, F32, kind="ExternalOutput").ap()
    else:
        for nm, shp in [("Fa_d", [128, TT]), ("Fb_d", [128, TT]),
                        ("Ga_d", [128, TT]), ("Gb_d", [128, TT])]:
            dr[nm] = nc.dram_tensor(nm, shp, F32, kind="ExternalInput").ap()
        dr["yout"] = nc.dram_tensor("yout", [1024, TT], F32, kind="ExternalOutput").ap()
    ctx = Ctx(nc, TT)
    defs = [("x", [128, 8 * TT], F32), ("tmp", [128, 512], F32),
            ("Fa", [128, TT], F32), ("Fb", [128, TT], F32),
            ("Ga", [128, TT], F32), ("Gb", [128, TT], F32)]
    if stage == "feat":
        defs += [("xb", [128, 8 * 512], BF16), ("qb", [128, 8 * 512], BF16),
                 ("kb", [128, 8 * 512], BF16), ("iqb", [128, 8 * 512], BF16),
                 ("ikb", [128, 8 * 512], BF16), ("tots", [128, 2], F32),
                 ("CFa", [128, 4], F32), ("CFb", [128, 4], F32),
                 ("RH", [128, 4], BF16), ("ybuf", [128, 4], BF16),
                 ("t2", [128, 4], F32), ("cbc", [128, 4], F32)]
    else:
        defs += [("CFa", [128, TT], F32), ("CFb", [128, TT], F32),
                 ("RH", [128, 4 * 512], BF16), ("ybuf", [128, 8 * 512], BF16),
                 ("t2", [128, 512], F32), ("cbc", [128, 2 * 512], F32),
                 ("rowtmp", [1, 512], F32), ("rows_sb", [1, 2 * TT + 2050], F32),
                 ("ucat_sb", [128, 1024], BF16), ("cbrow", [1, 1024], F32),
                 ("ab64", [64, 1], F32), ("zbc", [128, 1], F32),
                 ("G8_sb", [8, 256], F32), ("wsel_sb", [8, 1], F32),
                 ("carry_sb", [128, 2], F32), ("xb", [128, 8 * 512], BF16),
                 ("qb", [128, 8], BF16), ("kb", [128, 8], BF16),
                 ("iqb", [128, 8], BF16), ("ikb", [128, 8], BF16)]
    sb = _sbdefs(nc, defs)
    with nc.Block() as block:
        @block.sync
        def _(sync):
            s = ctx.s
            ctx.load_consts()
            pool = PsumPool(nc, 6)
            scr = LnScratch(nc)
            ldx = _ldx(ctx, s, sb["x"], dr["xin"])
            bufs = dict(sb)
            if stage == "feat":
                bufs["wq"] = WBuf(nc, "wwq", 8, 128)
                bufs["wk"] = WBuf(nc, "wwk", 8, 128)
                bufs["wi"] = WBuf(nc, "wwi", 8, 128)
                bufs["wf"] = WBuf(nc, "wwf", 8, 64)
                w = {k: dr[k] for k in ["WQT", "WKT", "WinvT", "Vb", "Wb",
                                        "Vt", "Wt", "Xt"]}
                w.update(Fa_d=dr["Fa_d"][:, :], Fb_d=dr["Fb_d"][:, :],
                         Ga_d=dr["Ga_d"][:, :], Gb_d=dr["Gb_d"][:, :],
                         tots_d=dr["tots_d"][:, :])
                toks = emit_l1attnb(ctx, _x_ap(sb["x"]), w, pool, scr, bufs,
                                    None, deps=[ldx], stage="feat")
                s.waitall("gpsimd", toks)
                s.waitall("sync", toks)
            else:
                lds = []
                for nm in ["Fa", "Fb", "Ga", "Gb"]:
                    lds.append(s.dma("sync", "f6", sb[nm][:], dr[nm + "_d"][:, :]))
                rows_tok = s.dma("sync", "r6", sb["rows_sb"][:], dr["rows"][:, :])
                ucat_tok = s.dma("gpsimd", "u6", sb["ucat_sb"][:], dr["Ucat"][:, :])
                ldg = s.dma("sync", "g6", sb["G8_sb"][:], dr["G8"][:, :])
                ldw = s.dma("sync", "g6b", sb["wsel_sb"][:], dr["wsel"][:, :])
                zt = s.op("vector", lambda v: v.memset(sb["zbc"][:], 0.0))
                rows = sb["rows_sb"]
                cinv_row = rows[0:1, 0:TT]
                cinv2_row = rows[0:1, TT:2 * TT]
                bias_b_row = rows[0:1, 2 * TT:2 * TT + 1024]
                bias_t_row = rows[0:1, 2 * TT + 1024:2 * TT + 2048]
                ab11 = rows[0:1, 2 * TT + 2048:2 * TT + 2049]
                at11 = rows[0:1, 2 * TT + 2049:2 * TT + 2050]
                t1 = s.op("vector", lambda v: v.tensor_scalar(
                    sb["cbrow"][0:1, :], bias_b_row, ab11, None, ALU.mult),
                    deps=[rows_tok])
                t2 = s.op("vector", lambda v: v.tensor_tensor(
                    sb["cbrow"][0:1, :], sb["cbrow"][0:1, :], bias_b_row,
                    ALU.add), deps=[t1])
                t3 = s.op("vector", lambda v: v.tensor_scalar(
                    sb["rowtmp"][0:1, 0:512], bias_t_row[0:1, 0:512], at11,
                    None, ALU.mult), deps=[rows_tok])
                t4 = s.op("vector", lambda v: v.tensor_tensor(
                    sb["cbrow"][0:1, 0:512], sb["cbrow"][0:1, 0:512],
                    sb["rowtmp"][0:1, 0:512], ALU.add), deps=[t2, t3])
                t5 = s.op("vector", lambda v: v.tensor_scalar(
                    sb["rowtmp"][0:1, 0:512], bias_t_row[0:1, 512:1024], at11,
                    None, ALU.mult), deps=[t4])
                cb_tok = s.op("vector", lambda v: v.tensor_tensor(
                    sb["cbrow"][0:1, 512:1024], sb["cbrow"][0:1, 512:1024],
                    sb["rowtmp"][0:1, 0:512], ALU.add), deps=[t5])
                ba, wa, sl = pool.get()
                m1 = s.op("tensor", lambda t: t.matmul(
                    ba[0:64, 0:1], ctx.ones_row[0:1, 0:64], ab11, start=True,
                    stop=True), deps=[rows_tok, wa] + ctx.const_toks)
                ab64_tok = s.op("vector", lambda v: v.tensor_copy(
                    sb["ab64"][:, 0:1], ba[0:64, 0:1]), deps=[m1])
                pool.done(sl, ab64_tok)
                cy = []
                for ch in range(2):
                    bc, wc, sc = pool.get()
                    mc_ = s.op("tensor", lambda t, ch=ch: t.matmul(
                        bc[:, 0:1], sb["G8_sb"][:, ch * 128:(ch + 1) * 128],
                        sb["wsel_sb"][:, 0:1], start=True, stop=True),
                        deps=[ldg, ldw, wc])
                    cyt = s.op("vector", lambda v, ch=ch: v.tensor_copy(
                        sb["carry_sb"][:, ch:ch + 1], bc[:, 0:1]), deps=[mc_])
                    pool.done(sc, cyt)
                    cy.append(cyt)
                bufs["wq"] = WBuf(nc, "wwq", 8, 128)
                bufs["zbc_tok"] = zt
                bufs["ucat_tok"] = ucat_tok
                bufs["cb_tok"] = cb_tok
                bufs["ab64_tok"] = ab64_tok
                bufs["rows_tok"] = rows_tok
                w = {"WOT": dr["WOT"], "cinv_row": cinv_row,
                     "cinv2_row": cinv2_row, "at11": at11, "rows_tok": rows_tok}

                def carry_fn(which):
                    col = sb["carry_sb"][:, 0:1] if which == "a" else sb["carry_sb"][:, 1:2]
                    return col, cy
                toks = emit_l1attnb(ctx, _x_ap(sb["x"]), w, pool, scr, bufs,
                                    carry_fn, deps=[ldx] + lds, stage="rest")
                _stx(ctx, s, dr["yout"], sb["x"], toks)
    return nc



def _run(nc, in_maps, outs):
    res = run_bass_kernel_spmd(nc, in_maps, core_ids=list(range(8)))
    return [{k: r[k] for k in outs} for r in res.results]


def _rotary_tables(positions, n_global):
    pos = np.maximum(np.asarray(positions, np.float64), 0.0)
    half = np.arange(0, 64, 2, dtype=np.float64)
    inv_freq = 1.0 / 10000.0 ** (half / 64)
    freqs = pos[:, None] * inv_freq[None, :]
    freqs = np.concatenate([freqs, freqs], -1)
    base = (half + 25.6) / 89.6
    power = (pos - n_global // 2) / 256.0
    scale = base[None, :] ** power[:, None]
    scale = np.concatenate([scale, scale], -1)
    cos, sin = np.cos(freqs), np.sin(freqs)
    return (np.ascontiguousarray(np.stack(
        [(cos * scale).T, (sin * scale).T, (cos / scale).T, (sin / scale).T],
        0)).astype(np.float32))


def _rperm():
    Rp = np.zeros((64, 64), np.float32)
    for i in range(32):
        Rp[i, i + 32] = -1.0
        Rp[i + 32, i] = 1.0
    return np.ascontiguousarray(Rp.T)


def _mkcols(vec, nch):
    out = np.zeros((128, nch), np.float32)
    for j in range(nch):
        seg = vec[j * 128:(j + 1) * 128]
        out[0:len(seg), j] = seg
    return out


def _c(a):
    return np.ascontiguousarray(a)


DBG = [None]


def kernel(**ins):
    def _dbg(tag, arrs):
        if DBG[0] is not None:
            DBG[0](tag, arrs)
    f32 = np.float32
    x0 = ins["L0_attn_a_x"]
    xc = []
    for c in range(8):
        b, q = _corepos(c)
        xc.append(_c(x0[b, q * TT:(q + 1) * TT, :].T.astype(f32)))
    ident = np.eye(128, dtype=f32)
    a = np.arange(128)
    mask0 = (a[:, None] >= a[None, :]).astype(f32)
    maskc = (a[:, None] <= a[None, :]).astype(f32)
    mask2 = (a[:, None] < a[None, :]).astype(f32)

    Wq, bq = ins["L0_attn_a_Wq"], ins["L0_attn_a_bq"]
    Wk, bk = ins["L0_attn_a_Wk"], ins["L0_attn_a_bk"]
    Wv, bv = ins["L0_attn_a_Wv"], ins["L0_attn_a_bv"]
    Wo, bo = ins["L0_attn_a_Wo"], ins["L0_attn_a_bo"]
    R = ins["L0_attn_a_R"]
    nc = build_s0_prod()
    common = {"Wq": _c(Wq), "Wk": _c(Wk), "Wv": _c(Wv),
              "WqT": _c(Wq.T), "WkT": _c(Wk.T),
              "RT": _c(np.concatenate([R.T, R.T], 0)),
              "b64q": _c(bq.reshape(16, 64).T), "b64k": _c(bk.reshape(16, 64).T),
              "bq": _c(bq.reshape(8, 128).T), "bk": _c(bk.reshape(8, 128).T),
              "bv": _c(bv.reshape(1, 1024)), "ident": ident}
    r = _run(nc, [dict(common, xin=xc[c]) for c in range(8)],
             ["K", "Q", "V", "ohQ", "ohK"])
    njS = SS // 128
    Kaug = np.zeros((NB, 16, 72, SS), BF)
    Vaug = np.ones((NB, 16, 128, njS * 65), BF)
    Qaug = np.zeros((8, 16, 72, TT), BF)
    for c in range(8):
        b, q = _corepos(c)
        sl = slice(q * TT, (q + 1) * TT)
        for h in range(16):
            Kaug[b, h, 0:64, sl] = r[c]["K"][h * 64:(h + 1) * 64]
            Kaug[b, h, 64:72, sl] = r[c]["ohK"][h * 8:(h + 1) * 8]
            Qaug[c, h, 0:64] = r[c]["Q"][h * 64:(h + 1) * 64]
            Qaug[c, h, 64:72] = r[c]["ohQ"][h * 8:(h + 1) * 8]
            Vh = np.asarray(r[c]["V"][:, h * 64:(h + 1) * 64]).reshape(16, 128, 64)
            for j in range(16):
                jj = q * 16 + j
                Vaug[b, h, :, jj * 65:jj * 65 + 64] = Vh[j]

    nc = build_s0_flash()
    im = []
    for c in range(8):
        b, q = _corepos(c)
        im.append({"Kaug": Kaug[b], "Vaug": Vaug[b], "Qaug": Qaug[c],
                   "Wo": _c(Wo), "bo": _c(bo.reshape(8, 128).T), "xin": xc[c]})
    r = _run(nc, im, ["yout"])
    xc = [r[c]["yout"] for c in range(8)]
    _dbg("s0", xc)

    s1_common = {"w1T": _c(ins["L0_ffn_w1_w"].T), "w2T": _c(ins["L0_ffn_w2_w"].T),
                 "h1T": _c(ins["L0_ffn_h1_w"].T), "h2T": _c(ins["L0_ffn_h2_w"].T),
                 "w3T": _c(ins["L0_ffn_w3_w"].T)}
    scl = np.zeros((128, 136), f32)
    scl[:, 0:32] = ins["L0_ffn_w1_s"].reshape(32, 128).T
    scl[:, 32:64] = ins["L0_ffn_w2_s"].reshape(32, 128).T
    scl[:, 64:96] = ins["L0_ffn_h1_s"].reshape(32, 128).T
    scl[:, 96:128] = ins["L0_ffn_h2_s"].reshape(32, 128).T
    scl[:, 128:136] = ins["L0_ffn_w3_s"].reshape(8, 128).T
    s1_common["scl"] = scl
    nc_s1 = build_s1()
    r = _run(nc_s1, [dict(s1_common, xin=xc[c]) for c in range(8)], ["yout"])
    xc = [r[c]["yout"] for c in range(8)]
    _dbg("s1", xc)

    nc = build_s2()
    s2c = {"KpT": _c(ins["L0_attn_b_key_param_tokens"].T),
           "Vp": _c(ins["L0_attn_b_value_param_tokens"])}
    r = _run(nc, [dict(s2c, xin=xc[c]) for c in range(8)], ["yout"])
    xc = [r[c]["yout"] for c in range(8)]
    _dbg("s2", xc)

    r = _run(nc_s1, [dict(s1_common, xin=xc[c]) for c in range(8)], ["yout"])
    xc = [r[c]["yout"] for c in range(8)]
    _dbg("s3", xc)

    nc = build_s4()
    s4c = {"WqT": _c(ins["L1_attn_a_Wq"]), "WkT": _c(ins["L1_attn_a_Wk"]),
           "WvT": _c(ins["L1_attn_a_Wv"]), "WoT": _c(ins["L1_attn_a_Wo"]),
           "bq": _c(ins["L1_attn_a_bq"].reshape(16, 64).T),
           "bk": _c(ins["L1_attn_a_bk"].reshape(2, 64).T),
           "bv": _c(ins["L1_attn_a_bv"].reshape(1, 128)),
           "bo": _c(ins["L1_attn_a_bo"].reshape(1, 1024)),
           "mask0": mask0, "mask2": mask2}
    im = []
    for c in range(8):
        b, q = _corepos(c)
        xh = np.zeros((1024, 256), f32)
        hm = np.zeros((1, 256), f32)
        hc = np.zeros((128, 2), f32)
        if q > 0:
            xh[:, 0:128] = xc[c - 1][:, TT - 128:TT]
            hm[0, 0:128] = 1.0
            hc[:, 0] = 1.0
        if q < 3:
            xh[:, 128:256] = xc[c + 1][:, 0:128]
            hm[0, 128:256] = 1.0
            hc[:, 1] = 1.0
        im.append(dict(s4c, xin=xc[c], xhalo=xh, hmrow=hm, hmcol=hc))
    r = _run(nc, im, ["yout"])
    xc = [r[c]["yout"] for c in range(8)]
    _dbg("s4", xc)

    wqkv = ins["L1_ffn_wqkv"]
    qsc, ksc = ins["L1_ffn_q_scale"], ins["L1_ffn_k_scale"]
    wo5 = ins["L1_ffn_wo"]
    lng, lnb_ = ins["L1_ffn_ln_g"], ins["L1_ffn_ln_b"]
    w1_, b1_ = ins["L1_ffn_w1"], ins["L1_ffn_b1"]
    w2_, b2_ = ins["L1_ffn_w2"], ins["L1_ffn_b2"]
    w2p = np.zeros((2, 2816, 1024), f32)
    w2p[:, 0:2730] = w2_
    rp = _rperm()
    rots = []
    for c in range(8):
        b, q = _corepos(c)
        pos0 = q * TT
        rots.append(_rotary_tables(np.arange(pos0 - 512, pos0 + TT), SS))
    nc_a = build_s5a(True)
    nc_a2 = build_s5a(False)
    nc_b = build_s5b(False)
    nc_b2 = build_s5b(True)
    cols5 = []
    for l in range(2):
        co = np.zeros((128, 68), f32)
        co[:, 0:22] = _mkcols(b1_[l, :2730], 22)
        co[:, 22:44] = _mkcols(b1_[l, 2730:], 22)
        co[:, 44:52] = b2_[l].reshape(8, 128).T
        co[:, 52:60] = lng[l].reshape(8, 128).T
        co[:, 60:68] = lnb_[l].reshape(8, 128).T
        cols5.append(co)
    rows5 = []
    for l in range(2):
        ro = np.zeros((1, 2 * 64 + 512), f32)
        ro[0, 0:64] = qsc[l]
        ro[0, 64:128] = ksc[l]
        rows5.append(ro)

    def run_s5(xcur):
        zh_src = xcur
        zc = None
        for l in range(2):
            rowl = rows5[l].copy()
            im = []
            for c in range(8):
                b, q = _corepos(c)
                rw = rowl.copy()
                rw[0, 128:640] = 0.0 if q > 0 else -1024.0
                if q == 0:
                    rw[0, 128:640] = -1024.0
                else:
                    rw[0, 128:640] = 0.0
                zh = np.zeros((1024, 516), f32)
                if q > 0:
                    src = (zh_src if l == 0 else zc)[c - 1]
                    zh[:, :] = src[:, TT - 516:TT]
                im.append({"zin": (zh_src if l == 0 else zc)[c], "zhalo": zh,
                           "wqkv": _c(wqkv[l]), "wo": _c(wo5[l]),
                           "rot": rots[c], "rperm": rp, "m0": mask0,
                           "mc": maskc, "rows": rw})
            r = _run(nc_a if l == 0 else nc_a2, im, ["zout"])
            zc = [r[c]["zout"] for c in range(8)]
            _dbg(f"s5a{l}", zc)
            last = l == 1
            im = []
            for c in range(8):
                b, q = _corepos(c)
                zm = np.zeros((128, 8), f32)
                if q > 0:
                    zm[:, :] = zc[c - 1][:, TT - 1].reshape(8, 128).T
                d = {"zin": zc[c], "zm1": zm, "w1": _c(w1_[l]),
                     "w2p": _c(w2p[l]), "cols": cols5[l]}
                if last:
                    d["xres"] = xcur[c]
                im.append(d)
            r = _run(nc_b2 if last else nc_b, im, ["zout"])
            zc = [r[c]["zout"] for c in range(8)]
            _dbg(f"s5b{l}", zc)
        return zc

    xc = run_s5(xc)

    nc = build_s6("feat")
    s6f = {"WQT": _c(ins["L1_attn_b_WQ"].T), "WKT": _c(ins["L1_attn_b_WK"].T),
           "WinvT": _c(ins["L1_attn_b_Winv"].T),
           "Vb": _c(ins["L1_attn_b_V_b"]), "Wb": _c(ins["L1_attn_b_W_b"]),
           "Vt": _c(ins["L1_attn_b_V_t"]), "Wt": _c(ins["L1_attn_b_W_t"]),
           "Xt": _c(ins["L1_attn_b_X_t"])}
    r = _run(nc, [dict(s6f, xin=xc[c]) for c in range(8)],
             ["Fa_d", "Fb_d", "Ga_d", "Gb_d", "tots_d"])
    G8 = np.zeros((8, 256), f32)
    for c in range(8):
        G8[c, 0:128] = r[c]["tots_d"][:, 0]
        G8[c, 128:256] = r[c]["tots_d"][:, 1]
    nc2 = build_s6("rest")
    ab = float(ins["L1_attn_b_alpha_bi"])
    at = float(ins["L1_attn_b_alpha_tri"])
    im = []
    for c in range(8):
        b, q = _corepos(c)
        counts = (q * TT + 1 + np.arange(TT)).astype(np.float64)
        rows = np.zeros((1, 2 * TT + 2050), f32)
        rows[0, 0:TT] = 1.0 / counts
        rows[0, TT:2 * TT] = 1.0 / counts ** 2
        rows[0, 2 * TT:2 * TT + 1024] = ins["L1_attn_b_bias_b"]
        rows[0, 2 * TT + 1024:2 * TT + 2048] = ins["L1_attn_b_bias_t"]
        rows[0, 2 * TT + 2048] = ab
        rows[0, 2 * TT + 2049] = at
        ws = np.zeros((8, 1), f32)
        for c2 in range(8):
            b2c, q2 = _corepos(c2)
            if b2c == b and q2 < q:
                ws[c2, 0] = 1.0
        im.append({"xin": xc[c], "WOT": _c(ins["L1_attn_b_WO"].T),
                   "Ucat": _c(np.concatenate([ins["L1_attn_b_U_b"].T,
                                              ins["L1_attn_b_U_t"].T], 0)),
                   "rows": rows, "G8": G8, "wsel": ws,
                   "Fa_d": r[c]["Fa_d"], "Fb_d": r[c]["Fb_d"],
                   "Ga_d": r[c]["Ga_d"], "Gb_d": r[c]["Gb_d"]})
    r2 = _run(nc2, im, ["yout"])
    xc = [r2[c]["yout"] for c in range(8)]
    _dbg("s6", xc)

    xc = run_s5(xc)

    out = np.zeros((2, SS, 1024), np.float32)
    for c in range(8):
        b, q = _corepos(c)
        out[b, q * TT:(q + 1) * TT, :] = xc[c].T
    return out
